# version 04
# baseline (speedup 1.0000x reference)
import numpy as np
import ml_dtypes
from contextlib import ExitStack

import concourse.bass as bass
import concourse.mybir as mybir
from concourse.bass_utils import run_bass_kernel_spmd

F32 = mybir.dt.float32
BF16 = mybir.dt.bfloat16
AF = mybir.ActivationFunctionType
ALU = mybir.AluOpType
AX = mybir.AxisListType.X

P = 128
NT = 16
TPC = P * NT
D = 1024
NCH = 8
E = 32
EI = 512
NCORES = 8
NUNITS = 34
NSLC = 24


def build_nc():
    nc = bass.Bass("TRN2", target_bir_lowering=False)

    d_x = nc.dram_tensor("xsh", [TPC, D], F32, kind="ExternalInput").ap()
    d_out = nc.dram_tensor("out", [TPC, D], F32, kind="ExternalOutput").ap()
    d_nw = nc.dram_tensor("attn_nw", [D], F32, kind="ExternalInput").ap()
    d_wqkv = nc.dram_tensor("attn_wqkv", [384, D], F32, kind="ExternalInput").ap()
    d_temp = nc.dram_tensor("attn_temp", [8], F32, kind="ExternalInput").ap()
    d_wout = nc.dram_tensor("attn_wout", [D, 128], F32, kind="ExternalInput").ap()
    d_gw = nc.dram_tensor("ffn_gw", [D, E], F32, kind="ExternalInput").ap()
    d_gbg = nc.dram_tensor("ffn_gb", [E], F32, kind="ExternalInput").ap()
    d_w1 = nc.dram_tensor("ffn_w1", [E, D, EI], F32, kind="ExternalInput").ap()
    d_b1 = nc.dram_tensor("ffn_b1", [E, EI], F32, kind="ExternalInput").ap()
    d_w3 = nc.dram_tensor("ffn_w3", [E, D, EI], F32, kind="ExternalInput").ap()
    d_b3 = nc.dram_tensor("ffn_b3", [E, EI], F32, kind="ExternalInput").ap()
    d_w2 = nc.dram_tensor("ffn_w2", [E, EI, D], F32, kind="ExternalInput").ap()
    d_b2 = nc.dram_tensor("ffn_b2", [E, D], F32, kind="ExternalInput").ap()
    d_sw1 = nc.dram_tensor("ffn_sw1", [D, D], F32, kind="ExternalInput").ap()
    d_sb1 = nc.dram_tensor("ffn_sb1", [D], F32, kind="ExternalInput").ap()
    d_sw3 = nc.dram_tensor("ffn_sw3", [D, D], F32, kind="ExternalInput").ap()
    d_sb3 = nc.dram_tensor("ffn_sb3", [D], F32, kind="ExternalInput").ap()
    d_sw2 = nc.dram_tensor("ffn_sw2", [D, D], F32, kind="ExternalInput").ap()
    d_sb2 = nc.dram_tensor("ffn_sb2", [D], F32, kind="ExternalInput").ap()
    d_c = nc.dram_tensor("cvec", [1, D], F32, kind="ExternalInput").ap()
    d_wmod = nc.dram_tensor("wmod", [D, 6 * D], F32, kind="ExternalInput").ap()

    cc_in = nc.dram_tensor("cc_in", [P, 130], F32)
    cc_out = nc.dram_tensor("cc_out", [P, 130], F32)

    c_idf = nc.inline_tensor(np.eye(P, dtype=np.float32), name="identf").ap()
    c_idb = nc.inline_tensor(np.eye(P, dtype=ml_dtypes.bfloat16), name="identb").ap()
    _bm = np.where(np.arange(P)[:, None] // 16 == np.arange(P)[None, :] // 16,
                   0.0, -1e30).astype(np.float32)
    c_bm = nc.inline_tensor(_bm, name="blkmask").ap()
    _sel = (np.arange(P)[None, :] // 16 == np.arange(8)[:, None]).astype(np.float32)
    c_sel = nc.inline_tensor(_sel, name="selc").ap()

    def sb(name, shape, dtype):
        return nc.alloc_sbuf_tensor(name, shape, dtype)

    xs = sb("xs", [P, NT, D], F32)
    fm = sb("fm", [P, NCH, TPC], BF16)
    h1 = sb("h1", [P, 4, TPC], BF16)
    s1 = sb("s1", [P, 2, 4, 512], BF16)
    xnb = sb("xnb", [P, 2, D], BF16)
    wA1 = sb("wA1", [P, NCH, EI], BF16)
    wA3 = sb("wA3", [P, NCH, EI], BF16)
    wA2 = sb("wA2", [P, 4, D], BF16)
    wB1 = sb("wB1", [P, NCH, EI], BF16)
    wB3 = sb("wB3", [P, NCH, EI], BF16)
    wB2 = sb("wB2", [P, 4, D], BF16)
    bcA = sb("bcA", [P, D], BF16)
    bcB = sb("bcB", [P, D], BF16)
    bcA2 = sb("bcA2", [P, D], BF16)
    bcB2 = sb("bcB2", [P, D], BF16)
    bcG = sb("bcG", [P, D], BF16)
    wqkvT = sb("wqkvT", [P, NCH, 384], BF16)
    woT = sb("woT", [P, D], BF16)
    gwsb = sb("gwsb", [P, NCH, E], BF16)
    b1T = sb("b1T", [P, 4, E], F32)
    b3T = sb("b3T", [P, 4, E], F32)
    sb1T = sb("sb1T", [P, 8], F32)
    sb3T = sb("sb3T", [P, 8], F32)
    nwc = sb("nwc", [P, 8], F32)
    b2gb = sb("b2gb", [E + 1, D], BF16)
    cw = sb("cw", [P, NT, E], F32)
    cwT = sb("cwT", [E + 1, TPC], BF16)
    gsc = sb("gsc", [P, E], F32)
    gmk = sb("gmk", [P, E], F32)
    gsel = sb("gsel", [P, E], F32)
    gs8 = sb("gs8", [P, 8], F32)
    gt8 = sb("gt8", [P, 8], F32)
    invm = sb("invm", [P, E], mybir.dt.uint32)
    negbig = sb("negbig", [P, E], F32)
    gbg_bc = sb("gbg_bc", [P, E], F32)
    gbg_row = sb("gbg_row", [1, E], F32)
    Smat = sb("Smat", [P, P], F32)
    Pbf = sb("Pbf", [P, P], BF16)
    PTb = sb("PTb", [P, P], BF16)
    blkm = sb("blkm", [P, P], F32)
    idf = sb("idf", [P, P], F32)
    idb = sb("idb", [P, P], BF16)
    selc = sb("selsb", [8, P], F32)
    ones_b = sb("ones_b", [1, P], BF16)
    ones_f = sb("ones_f", [1, P], F32)
    stg = sb("stg", [1, 512], F32)
    stgb = sb("stgb", [1, 512], BF16)
    cT = sb("cT", [P, 8], F32)
    t8 = sb("t8", [8, 1], F32)
    texp = sb("texp", [8, 1], F32)
    texpc = sb("texpc", [P, 1], F32)
    sm = sb("sm", [P, 1], F32)
    sqx = sb("sqx", [P, 1], F32)
    mcol = sb("mcol", [P, 1], F32)
    vcol = sb("vcol", [P, 1], F32)
    rcol = sb("rcol", [P, 1], F32)
    sqh = sb("sqh", [P, 1], F32)
    rr = sb("rr", [P, 1], F32)
    nmx = sb("nmx", [P, 1], F32)
    sden = sb("sden", [P, 1], F32)
    qn = sb("qn", [P, 1], F32)
    kn = sb("kn", [P, 1], F32)
    rq = sb("rq", [P, 1], F32)
    rk = sb("rk", [P, 1], F32)
    tcol = sb("tcol", [P, 1], F32)

    vfm = h1[:, 0, :]
    avfm = h1[:, 1, :]
    ht = h1[:, 2, :].bitcast(F32)
    crow = h1[64:65, 2, :].bitcast(F32)
    wmA = wA1.ap().rearrange("p a b -> p (a b)").bitcast(F32)
    wmB = wB1.ap().rearrange("p a b -> p (a b)").bitcast(F32)
    arst = h1[:, 3, :].bitcast(F32)[:, 0:130]
    aro = h1[:, 3, :].bitcast(F32)[:, 130:260]
    qkvsb = s1[:, 1, 0:2, 0:384]
    sqd = s1.ap().rearrange("p a b c -> p (a b c)").bitcast(F32)[:, 0:D]
    A1b = Pbf
    Pmat = Smat
    wo_stg = fm[:, 3, 0:D].rearrange("p (k o) -> p k o", o=P)

    ps = [nc.alloc_psum_tensor(f"ps{i}", [P, 512], F32) for i in range(8)]

    es = ExitStack()
    sems, cnt = {}, {}
    for n in ["lp", "lwA", "lwB", "luA", "luB", "la", "ls", "mt", "sl", "vt",
              "gf", "wm", "st", "cc"]:
        sems[n] = es.enter_context(nc.semaphore(n))
        cnt[n] = 0

    def inc(instr, name, amt=1):
        instr.then_inc(sems[name], amt)
        cnt[name] += amt
        return cnt[name]

    def wge(eng, name, val):
        eng.wait_ge(sems[name], val)

    MK = {}

    blk_es = ExitStack()
    block = blk_es.enter_context(nc.Block())

    @block.gpsimd
    def _(g):
        inc(g.dma_start(out=fm[:, 0, 0:D], in_=d_wqkv[0:128, :]), "lp", 16)
        inc(g.dma_start(out=fm[:, 1, 0:D], in_=d_wqkv[128:256, :]), "lp", 16)
        inc(g.dma_start(out=fm[:, 2, 0:D], in_=d_wqkv[256:384, :]), "lp", 16)
        inc(g.dma_start(out=wo_stg[:, :, :],
                        in_=d_wout.rearrange("(k p) o -> p k o", p=P)), "lp", 16)
        inc(g.dma_start(out=gwsb[:, :, :],
                        in_=d_gw.rearrange("(k p) e -> p k e", p=P)), "lp", 16)
        inc(g.dma_start(out=idb[:, :], in_=c_idb), "lp", 16)
    LD_PRO = cnt["lp"]

    @block.sync
    def _(sy):
        inc(sy.dma_start(out=xs[:, :, :],
                         in_=d_x.rearrange("(t p) c -> p t c", p=P)), "ls", 16)
        inc(sy.dma_start(out=idf[:, :], in_=c_idf), "ls", 16)
        inc(sy.dma_start(out=blkm[:, :], in_=c_bm), "ls", 16)
        inc(sy.dma_start(out=selc[:, :], in_=c_sel), "ls", 16)
        inc(sy.dma_start(out=crow[:, :], in_=d_c), "ls", 16)
        inc(sy.dma_start(out=t8[:, :],
                         in_=d_temp.rearrange("(a b) -> a b", b=1)), "ls", 16)
        inc(sy.dma_start(out=nwc[:, :], in_=d_nw.rearrange("(k p) -> p k", p=P),
                         allow_slow_non_contiguous=True), "ls", 16)
        inc(sy.dma_start(out=sb1T[:, :], in_=d_sb1.rearrange("(k p) -> p k", p=P),
                         allow_slow_non_contiguous=True), "ls", 16)
        inc(sy.dma_start(out=sb3T[:, :], in_=d_sb3.rearrange("(k p) -> p k", p=P),
                         allow_slow_non_contiguous=True), "ls", 16)
        inc(sy.dma_start(out=ht[0:E, 0:EI], in_=d_b1), "ls", 16)
        inc(sy.dma_start(out=ht[32:64, 0:EI], in_=d_b3), "ls", 16)
        inc(sy.dma_start(out=gbg_row[:, :],
                         in_=d_gbg.rearrange("(a e) -> a e", a=1)), "ls", 16)
    LS_PRO = cnt["ls"]

    wmod_marks = []

    @block.gpsimd
    def _(g):
        for s in range(NSLC):
            if s >= 2:
                wge(g, "wm", s - 1)
            buf = wmA if s % 2 == 0 else wmB
            sn = "lwA" if s % 2 == 0 else "lwB"
            i1 = g.dma_start(out=buf.rearrange("p (k i) -> p k i", i=256),
                             in_=d_wmod[:, 256 * s:256 * (s + 1)]
                             .rearrange("(k p) i -> p k i", p=P))
            wmod_marks.append((sn, inc(i1, sn, 16)))

    @block.vector
    def _(v):
        v.memset(ones_b[:, :], 1.0)
        v.memset(ones_f[:, :], 1.0)
        v.memset(negbig[:, :], -1e30)
        i1 = v.memset(cwT[E:E + 1, :], 1.0)
        inc(i1, "vt")
    V_MEMSET = cnt["vt"]

    @block.scalar
    def _(s):
        wge(s, "ls", LS_PRO)
        i1 = s.activation(crow[:, :], crow[:, :], AF.Silu)
        inc(i1, "sl")
        i1 = s.activation(texp[:, :], t8[:, :], AF.Exp)
        inc(i1, "sl")
    SL_SILUC = cnt["sl"]

    first_mk = {}

    @block.tensor
    def _(t):
        wge(t, "lp", LD_PRO)
        wge(t, "ls", LS_PRO)
        wge(t, "sl", SL_SILUC)
        wge(t, "vt", V_MEMSET)

    for ko in range(3):
        @block.tensor
        def _(t, ko=ko):
            for half in range(2):
                if ko >= 1:
                    wge(t, "vt", MK[("wqTev", ko - 1, half)])
                pb = ps[half]
                for q in range(4):
                    kc = half * 4 + q
                    t.matmul(pb[:, 128 * q:128 * (q + 1)],
                             fm[:, ko, 128 * kc:128 * (kc + 1)],
                             idb[:, :], start=True, stop=True)
                MK[("wqT", ko, half)] = inc(t.drain(), "mt")

        @block.vector
        def _(v, ko=ko):
            for half in range(2):
                wge(v, "mt", MK[("wqT", ko, half)])
                i1 = v.tensor_copy(
                    wqkvT[:, 4 * half:4 * (half + 1), 128 * ko:128 * (ko + 1)],
                    ps[half].ap().rearrange("p (k o) -> p k o", o=P))
                MK[("wqTev", ko, half)] = inc(i1, "vt")

    @block.tensor
    def _(t):
        for half in range(2):
            pb = ps[2 + half]
            for q in range(4):
                kc = half * 4 + q
                t.matmul(pb[:, 128 * q:128 * (q + 1)],
                         wo_stg[:, kc, :], idb[:, :], start=True, stop=True)
            MK[("woT", half)] = inc(t.drain(), "mt")
        for bi in range(2):
            pb = ps[4 + bi]
            src = ht[0:E, 0:EI] if bi == 0 else ht[32:64, 0:EI]
            ideye = idf[0:E, 0:E] if bi == 0 else idf[32:64, 32:64]
            for q in range(4):
                t.matmul(pb[:, E * q:E * (q + 1)],
                         src[:, 128 * q:128 * (q + 1)],
                         ideye, start=True, stop=True)
            MK[("bT", bi)] = inc(t.drain(), "mt")
        for kc in range(NCH):
            t.matmul(ps[6][:, kc:kc + 1], crow[0:1, 128 * kc:128 * (kc + 1)],
                     idf[64:65, 64:65], start=True, stop=True)
        t.matmul(ps[6][:, 8:9], selc[:, :], texp[:, :], start=True, stop=True)
        MK[("cT",)] = inc(t.drain(), "mt")

    @block.vector
    def _(v):
        for half in range(2):
            wge(v, "mt", MK[("woT", half)])
            v.tensor_copy(woT[:, 512 * half:512 * (half + 1)], ps[2 + half][:, :])
        for bi in range(2):
            wge(v, "mt", MK[("bT", bi)])
            v.tensor_copy((b1T if bi == 0 else b3T)[:, :, :],
                          ps[4 + bi][:, 0:128].rearrange("p (k e) -> p k e", e=E))
        wge(v, "mt", MK[("cT",)])
        v.tensor_copy(cT[:, :], ps[6][:, 0:8])
        v.tensor_copy(texpc[:, :], ps[6][:, 8:9])
        v.drain()
        for kc in range(NCH):
            i1 = v.tensor_scalar(out=wqkvT[:, kc, :], in0=wqkvT[:, kc, :],
                                 scalar1=nwc[:, kc:kc + 1], scalar2=None,
                                 op0=ALU.mult)
        inc(i1, "vt")
    V_CONSTS = cnt["vt"]

    bc_of_vec = {0: bcB, 1: bcA, 2: bcG, 3: bcB2, 4: bcA2, 5: bcG}
    stg_marks = []
    bc_marks = []

    for s in range(NSLC):
        vec, half = s // 4, s % 4
        is_gate = vec in (2, 5)
        is_ca = vec in (1, 4)

        @block.tensor
        def _(t, s=s):
            wge(t, wmod_marks[s][0], wmod_marks[s][1])
            if s == 0:
                wge(t, "vt", V_CONSTS)
            else:
                wge(t, "vt", stg_marks[s - 1])
            buf = wmA if s % 2 == 0 else wmB
            for kc in range(NCH):
                t.matmul(ps[2][0:1, 0:256], cT[:, kc:kc + 1],
                         buf[:, 256 * kc:256 * (kc + 1)],
                         start=(kc == 0), stop=(kc == NCH - 1))
            MK[("modmm", s)] = inc(t.drain(), "mt")
            inc(t.drain(), "wm")

        @block.vector
        def _(v, s=s, is_gate=is_gate, is_ca=is_ca):
            wge(v, "mt", MK[("modmm", s)])
            vw = stg[:, 0:256]
            v.tensor_copy(vw, ps[2][0:1, 0:256])
            v.drain()
            if is_gate:
                sgn = stg[:, 256:512]
                neg = h1[0:1, 3, :].bitcast(F32)[:, 0:256]
                v.tensor_scalar(out=sgn, in0=vw,
                                scalar1=0.0, scalar2=None, op0=ALU.is_ge)
                v.tensor_scalar(out=neg, in0=vw,
                                scalar1=-1.0, scalar2=None, op0=ALU.mult)
                v.drain()
                v.tensor_scalar(out=sgn, in0=sgn,
                                scalar1=2.0, scalar2=-1.0, op0=ALU.mult,
                                op1=ALU.add)
                v.tensor_tensor(vw, vw, neg, ALU.max)
                v.drain()
                v.tensor_scalar(out=vw, in0=vw, scalar1=0.25,
                                scalar2=None, op0=ALU.max)
                v.drain()
                v.tensor_tensor(vw, vw, sgn, ALU.mult)
                v.drain()
            if is_ca:
                v.tensor_scalar(out=vw, in0=vw, scalar1=1.0,
                                scalar2=None, op0=ALU.add)
                v.drain()
            i1 = v.tensor_copy(stgb[:, 0:256], vw)
            stg_marks.append(inc(i1, "vt"))

        @block.tensor
        def _(t, s=s):
            wge(t, "vt", stg_marks[s])
            if s >= 1:
                wge(t, "vt", bc_marks[s - 1])
            t.matmul(ps[3][:, 0:256], ones_b[0:1, 0:P], stgb[0:1, 0:256],
                     start=True, stop=True)
            MK[("modbc", s)] = inc(t.drain(), "mt")

        @block.vector
        def _(v, s=s, vec=vec, half=half):
            wge(v, "mt", MK[("modbc", s)])
            i1 = v.tensor_copy(bc_of_vec[vec][:, 256 * half:256 * (half + 1)],
                               ps[3][:, 0:256])
            bc_marks.append(inc(i1, "vt"))

        if s == 11:
            @block.vector
            def _(v):
                v.drain()
                i1 = v.tensor_tensor(woT[:, :], woT[:, :], bcG[:, :], ALU.mult)
                inc(i1, "vt")

    V_WOFOLD = cnt["vt"]
    M_END_MT = cnt["mt"]

    for tau in range(NT):
        @block.vector
        def _(v, tau=tau):
            if tau == 0:
                wge(v, "ls", LS_PRO)
                wge(v, "vt", V_WOFOLD)
            v.reduce_sum(sm[:, :], xs[:, tau, :], AX)
            i1 = v.drain()
            MK[("sm", tau)] = inc(i1, "vt")

        @block.scalar
        def _(s, tau=tau):
            if tau >= 1:
                wge(s, "vt", MK[("var", tau - 1)])
            i1 = s.activation(sqd, xs[:, tau, :], AF.Square,
                              accum_out=sqx[:, :])
            MK[("sqx", tau)] = inc(i1, "sl")

        @block.vector
        def _(v, tau=tau):
            wge(v, "sl", MK[("sqx", tau)])
            v.tensor_scalar(out=mcol[:, :], in0=sm[:, :], scalar1=1.0 / D,
                            scalar2=None, op0=ALU.mult)
            v.tensor_scalar(out=vcol[:, :], in0=sqx[:, :], scalar1=1.0 / D,
                            scalar2=None, op0=ALU.mult)
            v.drain()
            v.tensor_scalar(out=tcol[:, :], in0=mcol[:, :], scalar1=mcol[:, :],
                            scalar2=None, op0=ALU.mult)
            v.drain()
            v.tensor_scalar(out=vcol[:, :], in0=vcol[:, :], scalar1=tcol[:, :],
                            scalar2=1e-5, op0=ALU.subtract, op1=ALU.add)
            i1 = v.drain()
            MK[("var", tau)] = inc(i1, "vt")

        @block.scalar
        def _(s, tau=tau):
            wge(s, "vt", MK[("var", tau)])
            i1 = s.activation(tcol[:, :], vcol[:, :], AF.Sqrt)
            MK[("sd", tau)] = inc(i1, "sl")

        @block.vector
        def _(v, tau=tau):
            wge(v, "sl", MK[("sd", tau)])
            v.reciprocal(rcol[:, :], tcol[:, :])
            v.drain()
            v.tensor_scalar(out=ht[:, :], in0=xs[:, tau, :], scalar1=mcol[:, :],
                            scalar2=rcol[:, :], op0=ALU.subtract, op1=ALU.mult)
            v.drain()
            v.tensor_tensor(ht[:, :], ht[:, :], bcA[:, :], ALU.mult)
            v.drain()
            i1 = v.tensor_tensor(ht[:, :], ht[:, :], bcB[:, :], ALU.add)
            MK[("h", tau)] = inc(i1, "vt")

        @block.scalar
        def _(s, tau=tau):
            wge(s, "vt", MK[("h", tau)])
            i1 = s.activation(sqd, ht[:, :], AF.Square, accum_out=sqh[:, :])
            MK[("sqh", tau)] = inc(i1, "sl")

        @block.vector
        def _(v, tau=tau):
            wge(v, "sl", MK[("sqh", tau)])
            v.tensor_scalar(out=tcol[:, :], in0=sqh[:, :], scalar1=1.0 / D,
                            scalar2=1e-6, op0=ALU.mult, op1=ALU.add)
            i1 = v.drain()
            MK[("ms", tau)] = inc(i1, "vt")

        @block.scalar
        def _(s, tau=tau):
            wge(s, "vt", MK[("ms", tau)])
            i1 = s.activation(tcol[:, :], tcol[:, :], AF.Sqrt)
            MK[("sd2", tau)] = inc(i1, "sl")

        @block.vector
        def _(v, tau=tau):
            wge(v, "sl", MK[("sd2", tau)])
            v.reciprocal(rr[:, :], tcol[:, :])
            v.drain()
            if tau >= 2:
                wge(v, "mt", MK[("fmT", tau - 2)])
            i1 = v.tensor_scalar(out=xnb[:, tau % 2, :], in0=ht[:, :],
                                 scalar1=rr[:, :], scalar2=None, op0=ALU.mult)
            MK[("xn", tau)] = inc(i1, "vt")

        @block.tensor
        def _(t, tau=tau):
            wge(t, "vt", MK[("xn", tau)])
            if tau == 0:
                wge(t, "mt", M_END_MT)
            else:
                wge(t, "vt", MK[("fmev", tau - 1)])
            for q in range(4):
                t.matmul(ps[0][:, 128 * q:128 * (q + 1)],
                         xnb[:, tau % 2, 128 * q:128 * (q + 1)],
                         idb[:, :], start=True, stop=True)
            for q in range(4):
                t.matmul(ps[1][:, 128 * q:128 * (q + 1)],
                         xnb[:, tau % 2, 128 * (4 + q):128 * (5 + q)],
                         idb[:, :], start=True, stop=True)
            MK[("fmT", tau)] = inc(t.drain(), "mt")

        @block.vector
        def _(v, tau=tau):
            wge(v, "mt", MK[("fmT", tau)])
            v.tensor_copy(fm[:, 0:4, 128 * tau:128 * (tau + 1)],
                          ps[0].ap().rearrange("p (k t) -> p k t", t=P))
            i1 = v.tensor_copy(fm[:, 4:8, 128 * tau:128 * (tau + 1)],
                               ps[1].ap().rearrange("p (k t) -> p k t", t=P))
            MK[("fmev", tau)] = inc(i1, "vt")

        @block.tensor
        def _(t, tau=tau):
            wge(t, "vt", MK[("fmev", tau)])
            if tau >= 1:
                wge(t, "vt", MK[("qkev", tau - 1)])
            for kc in range(NCH):
                t.matmul(ps[2][:, 0:384], fm[:, kc, 128 * tau:128 * (tau + 1)],
                         wqkvT[:, kc, :], start=(kc == 0), stop=(kc == NCH - 1))
            MK[("qkv", tau)] = inc(t.drain(), "mt")

        @block.vector
        def _(v, tau=tau):
            wge(v, "mt", MK[("qkv", tau)])
            i1 = v.tensor_copy(qkvsb[:, tau % 2, :], ps[2][:, 0:384])
            MK[("qkev", tau)] = inc(i1, "vt")

        @block.tensor
        def _(t, tau=tau):
            wge(t, "vt", MK[("qkev", tau)])
            qk = qkvsb[:, tau % 2, :]
            t.matmul(ps[6][:, 0:256], qk[:, 0:128], qk[:, 0:256],
                     start=(tau == 0), stop=(tau == NT - 1))
            t.matmul(ps[3][:, 0:256], qk[:, 128:256], qk[:, 0:256],
                     start=(tau == 0), stop=(tau == NT - 1))
            if tau >= 2:
                wge(t, "vt", MK[("vev", tau - 2)])
            t.matmul(ps[4 + tau % 2][:, 0:128], qk[:, 256:384], idb[:, :],
                     start=True, stop=True)
            MK[("vT", tau)] = inc(t.drain(), "mt")

        @block.vector
        def _(v, tau=tau):
            wge(v, "mt", MK[("vT", tau)])
            i1 = v.tensor_copy(vfm[:, 128 * tau:128 * (tau + 1)],
                               ps[4 + tau % 2][:, 0:128])
            MK[("vev", tau)] = inc(i1, "vt")

    @block.vector
    def _(v):
        wge(v, "mt", MK[("vT", NT - 1)])
        v.tensor_tensor(Smat[:, :], ps[6][:, 0:128], idf[:, :], ALU.mult)
        v.drain()
        v.reduce_sum(qn[:, :], Smat[:, :], AX)
        v.drain()
        v.tensor_tensor(Smat[:, :], ps[3][:, 128:256], idf[:, :], ALU.mult)
        v.drain()
        v.reduce_sum(kn[:, :], Smat[:, :], AX)
        v.drain()
        v.tensor_copy(arst[:, 0:128], ps[3][:, 0:128])
        v.tensor_copy(arst[:, 128:129], qn[:, :])
        i1 = v.tensor_copy(arst[:, 129:130], kn[:, :])
        inc(i1, "vt")
    V_ARPACK = cnt["vt"]

    @block.gpsimd
    def _(g):
        wge(g, "vt", V_ARPACK)
        inc(g.dma_start(out=cc_in.ap(), in_=arst[:, :]), "la", 16)
        wge(g, "la", cnt["la"])
        i1 = g.collective_compute(
            "AllReduce", ALU.add,
            replica_groups=[list(range(NCORES))],
            ins=[cc_in.ap().opt()],
            outs=[cc_out.ap().opt()])
        inc(i1, "cc")
        wge(g, "cc", cnt["cc"])
        inc(g.dma_start(out=aro[:, :], in_=cc_out.ap()), "la", 16)
    LD_AR = cnt["la"]

    unit_ld_marks = []

    @block.gpsimd
    def _(g):
        for u in range(NUNITS):
            w1b, w3b, w2b = (wA1, wA3, wA2) if u % 2 == 0 else (wB1, wB3, wB2)
            if u < 2:
                wge(g, "wm", NSLC)
            else:
                wge(g, "gf", u - 1)
            if u < 2:
                src1 = d_sw1[:, EI * u:EI * (u + 1)]
                src3 = d_sw3[:, EI * u:EI * (u + 1)]
                src2 = d_sw2[EI * u:EI * (u + 1), :]
            else:
                e = u - 2
                src1, src3, src2 = d_w1[e], d_w3[e], d_w2[e]
            sn = "luA" if u % 2 == 0 else "luB"
            inc(g.dma_start(out=w1b[:, :, :],
                            in_=src1.rearrange("(k p) i -> p k i", p=P)), sn, 16)
            inc(g.dma_start(out=w3b[:, :, :],
                            in_=src3.rearrange("(k p) i -> p k i", p=P)), sn, 16)
            i1 = g.dma_start(out=w2b[:, :, :],
                             in_=src2.rearrange("(k p) j -> p k j", p=P))
            unit_ld_marks.append((sn, inc(i1, sn, 16)))

    @block.vector
    def _(v):
        wge(v, "la", LD_AR)
        v.tensor_copy(qn[:, :], aro[:, 128:129])
        i1 = v.tensor_copy(kn[:, :], aro[:, 129:130])
        v.drain()
        i1 = v.nop()
        inc(i1, "vt")
    V_ARQK = cnt["vt"]

    @block.scalar
    def _(s):
        wge(s, "vt", V_ARQK)
        s.activation(qn[:, :], qn[:, :], AF.Sqrt)
        i1 = s.activation(kn[:, :], kn[:, :], AF.Sqrt)
        inc(i1, "sl")
    SL_SQRTN = cnt["sl"]

    @block.vector
    def _(v):
        wge(v, "sl", SL_SQRTN)
        v.tensor_scalar_max(qn[:, :], qn[:, :], 1e-12)
        v.tensor_scalar_max(kn[:, :], kn[:, :], 1e-12)
        v.drain()
        v.reciprocal(rq[:, :], qn[:, :])
        v.reciprocal(rk[:, :], kn[:, :])
        v.drain()
        i1 = v.tensor_scalar(out=A1b[:, :], in0=aro[:, 0:128],
                             scalar1=rk[:, :], scalar2=None, op0=ALU.mult)
        inc(i1, "vt")
    V_A1 = cnt["vt"]

    @block.tensor
    def _(t):
        wge(t, "vt", V_A1)
        t.matmul(ps[7][:, 0:128], A1b[:, :], idb[:, :], start=True, stop=True)
        MK[("A1T",)] = inc(t.drain(), "mt")

    @block.vector
    def _(v):
        wge(v, "mt", MK[("A1T",)])
        v.tensor_scalar(out=Smat[:, :], in0=ps[7][:, 0:128], scalar1=rq[:, :],
                        scalar2=texpc[:, :], op0=ALU.mult, op1=ALU.mult)
        v.drain()
        v.tensor_tensor(Smat[:, :], Smat[:, :], blkm[:, :], ALU.add)
        v.drain()
        v.reduce_max(nmx[:, :], Smat[:, :], AX, negate=True)
        i1 = v.drain()
        inc(i1, "vt")
    V_SMAT = cnt["vt"]

    @block.scalar
    def _(s):
        wge(s, "vt", V_SMAT)
        i1 = s.activation(Pmat[:, :], Smat[:, :], AF.Exp, bias=nmx[:, :])
        inc(i1, "sl")
    SL_PEXP = cnt["sl"]

    @block.vector
    def _(v):
        wge(v, "sl", SL_PEXP)
        v.reduce_sum(sden[:, :], Pmat[:, :], AX)
        v.drain()
        v.reciprocal(sden[:, :], sden[:, :])
        v.drain()
        i1 = v.tensor_scalar(out=Pbf[:, :], in0=Pmat[:, :], scalar1=sden[:, :],
                             scalar2=None, op0=ALU.mult)
        inc(i1, "vt")
    V_PBF = cnt["vt"]

    @block.tensor
    def _(t):
        wge(t, "vt", V_PBF)
        t.matmul(ps[7][:, 0:128], Pbf[:, :], idb[:, :], start=True, stop=True)
        MK[("PT",)] = inc(t.drain(), "mt")

    @block.vector
    def _(v):
        wge(v, "mt", MK[("PT",)])
        i1 = v.tensor_copy(PTb[:, :], ps[7][:, 0:128])
        inc(i1, "vt")
    V_PT = cnt["vt"]

    for tc in range(4):
        @block.tensor
        def _(t, tc=tc):
            wge(t, "vt", V_PT)
            if tc >= 2:
                wge(t, "vt", MK[("avev", tc - 2)])
            t.matmul(ps[4 + tc % 2][:, :], PTb[:, :],
                     vfm[:, 512 * tc:512 * (tc + 1)], start=True, stop=True)
            MK[("av", tc)] = inc(t.drain(), "mt")

        @block.vector
        def _(v, tc=tc):
            wge(v, "mt", MK[("av", tc)])
            i1 = v.tensor_copy(avfm[:, 512 * tc:512 * (tc + 1)],
                               ps[4 + tc % 2][:, :])
            MK[("avev", tc)] = inc(i1, "vt")

    for tau in range(NT):
        @block.tensor
        def _(t, tau=tau):
            wge(t, "vt", MK[("avev", tau // 4)])
            if tau >= 1:
                wge(t, "vt", MK[("opev", tau - 1)])
            for cj in range(2):
                t.matmul(ps[cj][:, :], avfm[:, 128 * tau:128 * (tau + 1)],
                         woT[:, 512 * cj:512 * (cj + 1)], start=True, stop=True)
            MK[("op", tau)] = inc(t.drain(), "mt")

        @block.vector
        def _(v, tau=tau):
            wge(v, "mt", MK[("op", tau)])
            v.tensor_tensor(xs[:, tau, 0:512], xs[:, tau, 0:512],
                            ps[0][:, :], ALU.add)
            i1 = v.tensor_tensor(xs[:, tau, 512:1024], xs[:, tau, 512:1024],
                                 ps[1][:, :], ALU.add)
            MK[("opev", tau)] = inc(i1, "vt")

    @block.tensor
    def _(t):
        wge(t, "vt", MK[("opev", NT - 1)])
        t.matmul(ps[6][:, 0:E], ones_f[0:1, 0:P], gbg_row[0:1, :],
                 start=True, stop=True)
        MK[("gbg",)] = inc(t.drain(), "mt")

    @block.vector
    def _(v):
        wge(v, "mt", MK[("gbg",)])
        i1 = v.tensor_copy(gbg_bc[:, :], ps[6][:, 0:E])
        inc(i1, "vt")
    V_GBG = cnt["vt"]

    for tau in range(NT):
        @block.vector
        def _(v, tau=tau):
            if tau == 0:
                wge(v, "vt", V_GBG)
            v.reduce_sum(sm[:, :], xs[:, tau, :], AX)
            i1 = v.drain()
            MK[("sm2", tau)] = inc(i1, "vt")

        @block.scalar
        def _(s, tau=tau):
            if tau == 0:
                wge(s, "vt", MK[("opev", NT - 1)])
            else:
                wge(s, "vt", MK[("var2", tau - 1)])
            i1 = s.activation(sqd, xs[:, tau, :], AF.Square,
                              accum_out=sqx[:, :])
            MK[("sqx2", tau)] = inc(i1, "sl")

        @block.vector
        def _(v, tau=tau):
            wge(v, "sl", MK[("sqx2", tau)])
            v.tensor_scalar(out=mcol[:, :], in0=sm[:, :], scalar1=1.0 / D,
                            scalar2=None, op0=ALU.mult)
            v.tensor_scalar(out=vcol[:, :], in0=sqx[:, :], scalar1=1.0 / D,
                            scalar2=None, op0=ALU.mult)
            v.drain()
            v.tensor_scalar(out=tcol[:, :], in0=mcol[:, :], scalar1=mcol[:, :],
                            scalar2=None, op0=ALU.mult)
            v.drain()
            v.tensor_scalar(out=vcol[:, :], in0=vcol[:, :], scalar1=tcol[:, :],
                            scalar2=1e-5, op0=ALU.subtract, op1=ALU.add)
            i1 = v.drain()
            MK[("var2", tau)] = inc(i1, "vt")

        @block.scalar
        def _(s, tau=tau):
            wge(s, "vt", MK[("var2", tau)])
            i1 = s.activation(tcol[:, :], vcol[:, :], AF.Sqrt)
            MK[("sdb", tau)] = inc(i1, "sl")

        @block.vector
        def _(v, tau=tau):
            wge(v, "sl", MK[("sdb", tau)])
            v.reciprocal(rcol[:, :], tcol[:, :])
            v.drain()
            v.tensor_scalar(out=ht[:, :], in0=xs[:, tau, :], scalar1=mcol[:, :],
                            scalar2=rcol[:, :], op0=ALU.subtract, op1=ALU.mult)
            v.drain()
            v.tensor_tensor(ht[:, :], ht[:, :], bcA2[:, :], ALU.mult)
            v.drain()
            v.tensor_tensor(ht[:, :], ht[:, :], bcB2[:, :], ALU.add)
            v.drain()
            if tau >= 2:
                wge(v, "mt", MK[("fmT2", tau - 2)])
            i1 = v.tensor_copy(xnb[:, tau % 2, :], ht[:, :])
            MK[("h2", tau)] = inc(i1, "vt")

        @block.tensor
        def _(t, tau=tau):
            wge(t, "vt", MK[("h2", tau)])
            if tau >= 1:
                wge(t, "vt", MK[("fmev2", tau - 1)])
            for q in range(4):
                t.matmul(ps[0][:, 128 * q:128 * (q + 1)],
                         xnb[:, tau % 2, 128 * q:128 * (q + 1)],
                         idb[:, :], start=True, stop=True)
            for q in range(4):
                t.matmul(ps[1][:, 128 * q:128 * (q + 1)],
                         xnb[:, tau % 2, 128 * (4 + q):128 * (5 + q)],
                         idb[:, :], start=True, stop=True)
            MK[("fmT2", tau)] = inc(t.drain(), "mt")

        @block.vector
        def _(v, tau=tau):
            wge(v, "mt", MK[("fmT2", tau)])
            v.tensor_copy(fm[:, 0:4, 128 * tau:128 * (tau + 1)],
                          ps[0].ap().rearrange("p (k t) -> p k t", t=P))
            i1 = v.tensor_copy(fm[:, 4:8, 128 * tau:128 * (tau + 1)],
                               ps[1].ap().rearrange("p (k t) -> p k t", t=P))
            MK[("fmev2", tau)] = inc(i1, "vt")

        @block.tensor
        def _(t, tau=tau):
            wge(t, "vt", MK[("fmev2", tau)])
            if tau >= 1:
                wge(t, "vt", MK[("gtt", tau - 1)])
            for kc in range(NCH):
                t.matmul(ps[2][:, 0:E], fm[:, kc, 128 * tau:128 * (tau + 1)],
                         gwsb[:, kc, :], start=(kc == 0), stop=(kc == NCH - 1))
            MK[("gmm", tau)] = inc(t.drain(), "mt")

        @block.vector
        def _(v, tau=tau):
            wge(v, "mt", MK[("gmm", tau)])
            v.tensor_tensor(gsc[:, :], ps[2][:, 0:E], gbg_bc[:, :], ALU.add)
            v.drain()
            v.reduce_max(nmx[:, :], gsc[:, :], AX, negate=True)
            i1 = v.drain()
            MK[("gtt", tau)] = inc(i1, "vt")

        @block.scalar
        def _(s, tau=tau):
            wge(s, "vt", MK[("gtt", tau)])
            i1 = s.activation(gsc[:, :], gsc[:, :], AF.Exp, bias=nmx[:, :])
            MK[("gexp", tau)] = inc(i1, "sl")

        @block.vector
        def _(v, tau=tau):
            wge(v, "sl", MK[("gexp", tau)])
            v.reduce_sum(sden[:, :], gsc[:, :], AX)
            v.drain()
            v.reciprocal(sden[:, :], sden[:, :])
            v.drain()
            v.tensor_scalar(out=gsc[:, :], in0=gsc[:, :], scalar1=sden[:, :],
                            scalar2=None, op0=ALU.mult)
            v.drain()
            a0 = gsc.ap().rearrange("p (g f) -> p g f", f=4)
            first = True
            for (i, j) in [(0, 1), (0, 2), (0, 3), (1, 2), (1, 3), (2, 3)]:
                v.tensor_tensor(gt8[:, :], a0[:, :, i], a0[:, :, j], ALU.add)
                v.drain()
                if first:
                    v.tensor_copy(gs8[:, :], gt8[:, :])
                    first = False
                else:
                    v.tensor_tensor(gs8[:, :], gs8[:, :], gt8[:, :], ALU.max)
                v.drain()
            v.max(out=gt8[:, :], in_=gs8[:, :])
            v.drain()
            v.tensor_scalar(out=gs8[:, :], in0=gs8[:, :], scalar1=gt8[:, 3:4],
                            scalar2=None, op0=ALU.is_ge)
            v.drain()
            v.tensor_copy(gmk.ap().rearrange("p (g f) -> p g f", f=4),
                          gs8.ap().rearrange("p (g f) -> p g f", f=1)
                          .broadcast_to([P, 8, 4]))
            v.drain()
            v.tensor_scalar(out=invm[:, :], in0=gmk[:, :], scalar1=0.5,
                            scalar2=None, op0=ALU.is_lt)
            v.drain()
            v.tensor_copy(gmk[:, :], gsc[:, :])
            v.drain()
            v.copy_predicated(gmk[:, :], invm[:, :], negbig[:, :])
            v.drain()
            v.max(out=gt8[:, :], in_=gmk[:, :])
            v.drain()
            v.tensor_scalar(out=gsel[:, :], in0=gmk[:, :], scalar1=gt8[:, 3:4],
                            scalar2=None, op0=ALU.is_ge)
            v.drain()
            i1 = v.tensor_tensor(cw[:, tau, :], gsel[:, :], gsc[:, :], ALU.mult)
            MK[("cw", tau)] = inc(i1, "vt")

        @block.tensor
        def _(t, tau=tau):
            wge(t, "vt", MK[("cw", tau)])
            if tau >= 2:
                wge(t, "vt", MK[("cwTev", tau - 2)])
            t.matmul(ps[3 + tau % 2][0:E, 0:128], cw[:, tau, :], idf[:, :],
                     start=True, stop=True)
            MK[("cwT", tau)] = inc(t.drain(), "mt")

        @block.vector
        def _(v, tau=tau):
            wge(v, "mt", MK[("cwT", tau)])
            i1 = v.tensor_copy(cwT[0:E, 128 * tau:128 * (tau + 1)],
                               ps[3 + tau % 2][0:E, 0:128])
            MK[("cwTev", tau)] = inc(i1, "vt")

    @block.sync
    def _(sy):
        wge(sy, "vt", MK[("h2", NT - 1)])
        inc(sy.dma_start(out=ht[0:E, :], in_=d_b2), "ls", 16)
        i1 = sy.dma_start(out=ht[E:E + 1, :],
                          in_=d_sb2.rearrange("(a c) -> a c", a=1))
        inc(i1, "ls", 16)
    LS_B2 = cnt["ls"]

    @block.vector
    def _(v):
        wge(v, "ls", LS_B2)
        wge(v, "vt", bc_marks[23])
        i1 = v.tensor_tensor(b2gb[:, :], ht[0:E + 1, :], bcG[0:E + 1, :],
                             ALU.mult)
        inc(i1, "vt")
    V_B2G = cnt["vt"]
    C_END_VT = cnt["vt"]

    setA = [ps[0], ps[1], ps[2], ps[3]]
    setB = [ps[4], ps[5], ps[6], ps[7]]

    def unit_bufs(u):
        return (wA1, wA3, wA2) if u % 2 == 0 else (wB1, wB3, wB2)

    def emit_w1(u, i):
        w1b, _, _ = unit_bufs(u)

        @block.tensor
        def _(t):
            st = setA if i % 2 == 0 else setB
            if i == 0:
                wge(t, unit_ld_marks[u][0], unit_ld_marks[u][1])
            if u == 0 and i < 2:
                wge(t, "vt", C_END_VT)
            elif i < 2:
                wge(t, "vt", MK[("uend", u - 1)])
            else:
                wge(t, "vt", MK[("tt", u, i - 2)])
            for c in range(NCH):
                for tcc in range(4):
                    t.matmul(st[tcc][:, :], w1b[:, c, 128 * i:128 * (i + 1)],
                             fm[:, c, 512 * tcc:512 * (tcc + 1)],
                             start=(c == 0), stop=(c == NCH - 1))
            MK[("w1", u, i)] = inc(t.drain(), "mt")

    def emit_silu(u, i):
        shared = u < 2

        @block.scalar
        def _(s):
            st0 = 0 if i % 2 == 0 else 4
            wge(s, "mt", MK[("w1", u, i)])
            if i >= 2:
                wge(s, "vt", MK[("tt", u, i - 2)])
            elif u >= 1:
                wge(s, "vt", MK[("tt", u - 1, i + 2)])
            bias = (sb1T[:, 4 * u + i:4 * u + i + 1] if shared
                    else b1T[:, i, (u - 2):(u - 1)])
            for tcc in range(4):
                i1 = s.activation(s1[:, i % 2, tcc, :], ps[st0 + tcc][:, :],
                                  AF.Silu, bias=bias)
            MK[("si", u, i)] = inc(i1, "sl")

    def emit_w3(u, i):
        _, w3b, _ = unit_bufs(u)

        @block.tensor
        def _(t):
            st = setA if i % 2 == 0 else setB
            wge(t, "sl", MK[("si", u, i)])
            for c in range(NCH):
                for tcc in range(4):
                    t.matmul(st[tcc][:, :], w3b[:, c, 128 * i:128 * (i + 1)],
                             fm[:, c, 512 * tcc:512 * (tcc + 1)],
                             start=(c == 0), stop=(c == NCH - 1))
            MK[("w3", u, i)] = inc(t.drain(), "mt")

    def emit_tt(u, i):
        shared = u < 2

        @block.vector
        def _(v):
            st0 = 0 if i % 2 == 0 else 4
            wge(v, "mt", MK[("w3", u, i)])
            if u >= 1 and i == 0:
                wge(v, "mt", MK[("w2", u - 1, NT - 1)])
            b3c = (sb3T[:, 4 * u + i:4 * u + i + 1] if shared
                   else b3T[:, i, (u - 2):(u - 1)])
            for tcc in range(4):
                i1 = v.scalar_tensor_tensor(
                    out=h1[:, i, 512 * tcc:512 * (tcc + 1)],
                    in0=ps[st0 + tcc][:, :], scalar=b3c,
                    in1=s1[:, i % 2, tcc, :], op0=ALU.add, op1=ALU.mult)
            MK[("tt", u, i)] = inc(i1, "vt")

    def emit_w2fold(u):
        _, _, w2b = unit_bufs(u)

        @block.vector
        def _(v):
            wge(v, unit_ld_marks[u][0], unit_ld_marks[u][1])
            if u == 0:
                wge(v, "vt", V_B2G)
            for ki in range(4):
                i1 = v.tensor_tensor(w2b[:, ki, :], w2b[:, ki, :], bcG[:, :],
                                     ALU.mult)
            MK[("w2f", u)] = inc(i1, "vt")

    def emit_w2_group(u, g):
        _, _, w2b = unit_bufs(u)

        @block.tensor
        def _(t):
            if g == 0:
                wge(t, "vt", MK[("tt", u, 3)])
                wge(t, "vt", MK[("w2f", u)])
            for tau in range(2 * g, 2 * g + 2):
                pa = ps[0 + 2 * (tau % 2)]
                pb = ps[1 + 2 * (tau % 2)]
                if tau >= 2:
                    wge(t, "vt", MK[("ev", u, tau - 2)])
                for i in range(4):
                    t.matmul(pa[:, :], h1[:, i, 128 * tau:128 * (tau + 1)],
                             w2b[:, i, 0:512], start=(i == 0), stop=(i == 3))
                    t.matmul(pb[:, :], h1[:, i, 128 * tau:128 * (tau + 1)],
                             w2b[:, i, 512:1024], start=(i == 0), stop=(i == 3))
                MK[("w2", u, tau)] = inc(t.drain(), "mt")
            if g == 7:
                inc(t.drain(), "gf")

    def emit_ev_group(u, g):
        shared = u < 2

        @block.vector
        def _(v):
            for tau in range(2 * g, 2 * g + 2):
                pa = ps[0 + 2 * (tau % 2)]
                pb = ps[1 + 2 * (tau % 2)]
                wge(v, "mt", MK[("w2", u, tau)])
                if shared:
                    v.tensor_tensor(xs[:, tau, 0:512], xs[:, tau, 0:512],
                                    pa[:, :], ALU.add)
                    i1 = v.tensor_tensor(xs[:, tau, 512:1024],
                                         xs[:, tau, 512:1024], pb[:, :],
                                         ALU.add)
                else:
                    cws = cw[:, tau, (u - 2):(u - 1)]
                    v.scalar_tensor_tensor(
                        out=xs[:, tau, 0:512], in0=pa[:, :], scalar=cws,
                        in1=xs[:, tau, 0:512], op0=ALU.mult, op1=ALU.add)
                    i1 = v.scalar_tensor_tensor(
                        out=xs[:, tau, 512:1024], in0=pb[:, :], scalar=cws,
                        in1=xs[:, tau, 512:1024], op0=ALU.mult, op1=ALU.add)
                MK[("ev", u, tau)] = inc(i1, "vt")
            if g == 7:
                MK[("uend", u)] = cnt["vt"]

    for u in range(NUNITS):
        emit_w2fold(u)
        emit_w1(u, 0)
        emit_silu(u, 0)
        emit_w1(u, 1)
        emit_silu(u, 1)
        emit_w3(u, 0)
        emit_tt(u, 0)
        emit_w1(u, 2)
        emit_silu(u, 2)
        emit_w3(u, 1)
        emit_tt(u, 1)
        emit_w1(u, 3)
        emit_silu(u, 3)
        emit_w3(u, 2)
        emit_tt(u, 2)
        emit_w3(u, 3)
        emit_tt(u, 3)
        for g in range(8):
            emit_w2_group(u, g)
            emit_ev_group(u, g)

    for g in range(8):
        @block.tensor
        def _(t, g=g):
            if g == 0:
                wge(t, "vt", MK[("cwTev", NT - 1)])
                wge(t, "vt", V_B2G)
            for tau in range(2 * g, 2 * g + 2):
                pa = ps[0 + 2 * (tau % 2)]
                pb = ps[1 + 2 * (tau % 2)]
                if tau < 2:
                    wge(t, "vt", MK[("uend", NUNITS - 1)])
                else:
                    wge(t, "vt", MK[("b2ev", tau - 2)])
                t.matmul(pa[:, :], cwT[:, 128 * tau:128 * (tau + 1)],
                         b2gb[:, 0:512], start=True, stop=True)
                t.matmul(pb[:, :], cwT[:, 128 * tau:128 * (tau + 1)],
                         b2gb[:, 512:1024], start=True, stop=True)
                MK[("b2mm", tau)] = inc(t.drain(), "mt")

        @block.vector
        def _(v, g=g):
            for tau in range(2 * g, 2 * g + 2):
                pa = ps[0 + 2 * (tau % 2)]
                pb = ps[1 + 2 * (tau % 2)]
                wge(v, "mt", MK[("b2mm", tau)])
                v.tensor_tensor(xs[:, tau, 0:512], xs[:, tau, 0:512],
                                pa[:, :], ALU.add)
                i1 = v.tensor_tensor(xs[:, tau, 512:1024],
                                     xs[:, tau, 512:1024], pb[:, :], ALU.add)
                MK[("b2ev", tau)] = inc(i1, "vt")
    V_FINAL = cnt["vt"]

    @block.sync
    def _(sy):
        wge(sy, "vt", V_FINAL)
        inc(sy.dma_start(out=d_out.rearrange("(t p) c -> p t c", p=P),
                         in_=xs[:, :, :]), "st", 16)
        wge(sy, "st", cnt["st"])

    blk_es.close()
    es.close()
    return nc


_CACHE = {}


def kernel(**inputs):
    nc = _CACHE.get("nc")
    if nc is None:
        nc = build_nc()
        _CACHE["nc"] = nc

    x = np.ascontiguousarray(
        np.asarray(inputs["L0_attn_x"], dtype=np.float32).reshape(16384, D))
    full = {
        "attn_nw": np.ascontiguousarray(inputs["L0_attn_norm_weight"], dtype=np.float32),
        "attn_wqkv": np.ascontiguousarray(inputs["L0_attn_w_qkv"], dtype=np.float32),
        "attn_temp": np.ascontiguousarray(
            np.asarray(inputs["L0_attn_temperature"], dtype=np.float32).reshape(8)),
        "attn_wout": np.ascontiguousarray(inputs["L0_attn_w_out"], dtype=np.float32),
        "ffn_gw": np.ascontiguousarray(inputs["L0_ffn_gw"], dtype=np.float32),
        "ffn_gb": np.ascontiguousarray(inputs["L0_ffn_gb"], dtype=np.float32),
        "ffn_w1": np.ascontiguousarray(inputs["L0_ffn_w1"], dtype=np.float32),
        "ffn_b1": np.ascontiguousarray(inputs["L0_ffn_b1"], dtype=np.float32),
        "ffn_w3": np.ascontiguousarray(inputs["L0_ffn_w3"], dtype=np.float32),
        "ffn_b3": np.ascontiguousarray(inputs["L0_ffn_b3"], dtype=np.float32),
        "ffn_w2": np.ascontiguousarray(inputs["L0_ffn_w2"], dtype=np.float32),
        "ffn_b2": np.ascontiguousarray(inputs["L0_ffn_b2"], dtype=np.float32),
        "ffn_sw1": np.ascontiguousarray(inputs["L0_ffn_sw1"], dtype=np.float32),
        "ffn_sb1": np.ascontiguousarray(inputs["L0_ffn_sb1"], dtype=np.float32),
        "ffn_sw3": np.ascontiguousarray(inputs["L0_ffn_sw3"], dtype=np.float32),
        "ffn_sb3": np.ascontiguousarray(inputs["L0_ffn_sb3"], dtype=np.float32),
        "ffn_sw2": np.ascontiguousarray(inputs["L0_ffn_sw2"], dtype=np.float32),
        "ffn_sb2": np.ascontiguousarray(inputs["L0_ffn_sb2"], dtype=np.float32),
        "cvec": np.ascontiguousarray(
            np.asarray(inputs["L0_c"], dtype=np.float32).reshape(1, D)),
        "wmod": np.ascontiguousarray(inputs["L0_wmod"], dtype=np.float32),
    }
    in_maps = []
    for c in range(NCORES):
        m = dict(full)
        m["xsh"] = np.ascontiguousarray(x[c * TPC:(c + 1) * TPC])
        in_maps.append(m)
    res = run_bass_kernel_spmd(nc, in_maps, core_ids=list(range(NCORES)))
    out = np.concatenate([np.asarray(res.results[c]["out"])
                          for c in range(NCORES)], axis=0)
    return out.reshape(1, 16384, D)


# version 05
# speedup vs baseline: 1.0642x; 1.0642x over previous
import numpy as np
import ml_dtypes
from contextlib import ExitStack

import concourse.bass as bass
import concourse.mybir as mybir
from concourse.bass_utils import run_bass_kernel_spmd

F32 = mybir.dt.float32
BF16 = mybir.dt.bfloat16
AF = mybir.ActivationFunctionType
ALU = mybir.AluOpType
AX = mybir.AxisListType.X

P = 128
NT = 16
TPC = P * NT
D = 1024
NCH = 8
E = 32
EI = 512
NCORES = 8
NUNITS = 34
NSLC = 24


def build_nc():
    nc = bass.Bass("TRN2", target_bir_lowering=False)

    d_x = nc.dram_tensor("xsh", [TPC, D], F32, kind="ExternalInput").ap()
    d_out = nc.dram_tensor("out", [TPC, D], F32, kind="ExternalOutput").ap()
    d_nw = nc.dram_tensor("attn_nw", [D], F32, kind="ExternalInput").ap()
    d_wqkv = nc.dram_tensor("attn_wqkv", [384, D], F32, kind="ExternalInput").ap()
    d_temp = nc.dram_tensor("attn_temp", [8], F32, kind="ExternalInput").ap()
    d_wout = nc.dram_tensor("attn_wout", [D, 128], F32, kind="ExternalInput").ap()
    d_gw = nc.dram_tensor("ffn_gw", [D, E], F32, kind="ExternalInput").ap()
    d_gbg = nc.dram_tensor("ffn_gb", [E], F32, kind="ExternalInput").ap()
    d_w1 = nc.dram_tensor("ffn_w1", [E, D, EI], F32, kind="ExternalInput").ap()
    d_b1 = nc.dram_tensor("ffn_b1", [E, EI], F32, kind="ExternalInput").ap()
    d_w3 = nc.dram_tensor("ffn_w3", [E, D, EI], F32, kind="ExternalInput").ap()
    d_b3 = nc.dram_tensor("ffn_b3", [E, EI], F32, kind="ExternalInput").ap()
    d_w2 = nc.dram_tensor("ffn_w2", [E, EI, D], F32, kind="ExternalInput").ap()
    d_b2 = nc.dram_tensor("ffn_b2", [E, D], F32, kind="ExternalInput").ap()
    d_sw1 = nc.dram_tensor("ffn_sw1", [D, D], F32, kind="ExternalInput").ap()
    d_sb1 = nc.dram_tensor("ffn_sb1", [D], F32, kind="ExternalInput").ap()
    d_sw3 = nc.dram_tensor("ffn_sw3", [D, D], F32, kind="ExternalInput").ap()
    d_sb3 = nc.dram_tensor("ffn_sb3", [D], F32, kind="ExternalInput").ap()
    d_sw2 = nc.dram_tensor("ffn_sw2", [D, D], F32, kind="ExternalInput").ap()
    d_sb2 = nc.dram_tensor("ffn_sb2", [D], F32, kind="ExternalInput").ap()
    d_c = nc.dram_tensor("cvec", [1, D], F32, kind="ExternalInput").ap()
    d_wmod = nc.dram_tensor("wmod", [D, 6 * D], F32, kind="ExternalInput").ap()

    cc_in = nc.dram_tensor("cc_in", [P, 130], F32)
    cc_out = nc.dram_tensor("cc_out", [P, 130], F32)

    c_idf = nc.inline_tensor(np.eye(P, dtype=np.float32), name="identf").ap()
    c_idb = nc.inline_tensor(np.eye(P, dtype=ml_dtypes.bfloat16), name="identb").ap()
    _bm = np.where(np.arange(P)[:, None] // 16 == np.arange(P)[None, :] // 16,
                   0.0, -1e30).astype(np.float32)
    c_bm = nc.inline_tensor(_bm, name="blkmask").ap()
    _sel = (np.arange(P)[None, :] // 16 == np.arange(8)[:, None]).astype(np.float32)
    c_sel = nc.inline_tensor(_sel, name="selc").ap()

    def sb(name, shape, dtype):
        return nc.alloc_sbuf_tensor(name, shape, dtype)

    xs = sb("xs", [P, NT, D], F32)
    fm = sb("fm", [P, NCH, TPC], BF16)
    h1 = sb("h1", [P, 4, TPC], BF16)
    s1 = sb("s1", [P, 2, 4, 512], BF16)
    xnb = sb("xnb", [P, 2, D], BF16)
    wA1 = sb("wA1", [P, NCH, EI], BF16)
    wA3 = sb("wA3", [P, NCH, EI], BF16)
    wA2 = sb("wA2", [P, 4, D], BF16)
    wB1 = sb("wB1", [P, NCH, EI], BF16)
    wB3 = sb("wB3", [P, NCH, EI], BF16)
    wB2 = sb("wB2", [P, 4, D], BF16)
    bcA = sb("bcA", [P, D], BF16)
    bcB = sb("bcB", [P, D], BF16)
    bcA2 = sb("bcA2", [P, D], BF16)
    bcB2 = sb("bcB2", [P, D], BF16)
    bcG = sb("bcG", [P, D], BF16)
    wqkvT = sb("wqkvT", [P, NCH, 384], BF16)
    woT = sb("woT", [P, D], BF16)
    gwsb = sb("gwsb", [P, NCH, E], BF16)
    b1T = sb("b1T", [P, 4, E], F32)
    b3T = sb("b3T", [P, 4, E], F32)
    sb1T = sb("sb1T", [P, 8], F32)
    sb3T = sb("sb3T", [P, 8], F32)
    nwc = sb("nwc", [P, 8], F32)
    b2gb = sb("b2gb", [E + 1, D], BF16)
    cw = sb("cw", [P, NT, E], F32)
    cwT = sb("cwT", [E + 1, TPC], BF16)
    gsc = sb("gsc", [P, E], F32)
    gmk = sb("gmk", [P, E], F32)
    gsel = sb("gsel", [P, E], F32)
    gs8 = sb("gs8", [P, 8], F32)
    gt8 = sb("gt8", [P, 8], F32)
    invm = sb("invm", [P, E], mybir.dt.uint32)
    negbig = sb("negbig", [P, E], F32)
    gbg_bc = sb("gbg_bc", [P, E], F32)
    gbg_row = sb("gbg_row", [1, E], F32)
    Smat = sb("Smat", [P, P], F32)
    Pbf = sb("Pbf", [P, P], BF16)
    PTb = sb("PTb", [P, P], BF16)
    blkm = sb("blkm", [P, P], F32)
    idf = sb("idf", [P, P], F32)
    idb = sb("idb", [P, P], BF16)
    selc = sb("selsb", [8, P], F32)
    ones_b = sb("ones_b", [1, P], BF16)
    ones_f = sb("ones_f", [1, P], F32)
    stg = sb("stg", [1, 512], F32)
    stgb = sb("stgb", [1, 512], BF16)
    cT = sb("cT", [P, 8], F32)
    t8 = sb("t8", [8, 1], F32)
    texp = sb("texp", [8, 1], F32)
    texpc = sb("texpc", [P, 1], F32)
    sm = sb("sm", [P, 1], F32)
    sqx = sb("sqx", [P, 1], F32)
    mcol = sb("mcol", [P, 1], F32)
    vcol = sb("vcol", [P, 1], F32)
    rcol = sb("rcol", [P, 1], F32)
    sqh = sb("sqh", [P, 1], F32)
    rr = sb("rr", [P, 1], F32)
    nmx = sb("nmx", [P, 1], F32)
    sden = sb("sden", [P, 1], F32)
    qn = sb("qn", [P, 1], F32)
    kn = sb("kn", [P, 1], F32)
    rq = sb("rq", [P, 1], F32)
    rk = sb("rk", [P, 1], F32)
    tcol = sb("tcol", [P, 1], F32)

    vfm = h1[:, 0, :]
    avfm = h1[:, 1, :]
    ht = h1[:, 2, :].bitcast(F32)
    crow = h1[64:65, 2, :].bitcast(F32)
    wmA = wA1.ap().rearrange("p a b -> p (a b)").bitcast(F32)
    wmB = wB1.ap().rearrange("p a b -> p (a b)").bitcast(F32)
    arst = h1[:, 3, :].bitcast(F32)[:, 0:130]
    aro = h1[:, 3, :].bitcast(F32)[:, 130:260]
    qkvsb = s1[:, 1, 0:2, 0:384]
    sqd = s1.ap().rearrange("p a b c -> p (a b c)").bitcast(F32)[:, 0:D]
    A1b = Pbf
    Pmat = Smat
    wo_stg = fm[:, 3, 0:D].rearrange("p (k o) -> p k o", o=P)

    ps = [nc.alloc_psum_tensor(f"ps{i}", [P, 512], F32) for i in range(8)]

    es = ExitStack()
    sems, cnt = {}, {}
    for n in ["lp", "lwA", "lwB", "luA", "luB", "la", "ls", "mt", "sl", "vt",
              "gf", "wm", "st", "cc"]:
        sems[n] = es.enter_context(nc.semaphore(n))
        cnt[n] = 0

    def inc(instr, name, amt=1):
        instr.then_inc(sems[name], amt)
        cnt[name] += amt
        return cnt[name]

    def wge(eng, name, val):
        eng.wait_ge(sems[name], val)

    MK = {}

    blk_es = ExitStack()
    block = blk_es.enter_context(nc.Block())

    @block.gpsimd
    def _(g):
        inc(g.dma_start(out=fm[:, 0, 0:D], in_=d_wqkv[0:128, :]), "lp", 16)
        inc(g.dma_start(out=fm[:, 1, 0:D], in_=d_wqkv[128:256, :]), "lp", 16)
        inc(g.dma_start(out=fm[:, 2, 0:D], in_=d_wqkv[256:384, :]), "lp", 16)
        inc(g.dma_start(out=wo_stg[:, :, :],
                        in_=d_wout.rearrange("(k p) o -> p k o", p=P)), "lp", 16)
        inc(g.dma_start(out=gwsb[:, :, :],
                        in_=d_gw.rearrange("(k p) e -> p k e", p=P)), "lp", 16)
        inc(g.dma_start(out=idb[:, :], in_=c_idb), "lp", 16)
    LD_PRO = cnt["lp"]

    @block.sync
    def _(sy):
        inc(sy.dma_start(out=xs[:, :, :],
                         in_=d_x.rearrange("(t p) c -> p t c", p=P)), "ls", 16)
        inc(sy.dma_start(out=idf[:, :], in_=c_idf), "ls", 16)
        inc(sy.dma_start(out=blkm[:, :], in_=c_bm), "ls", 16)
        inc(sy.dma_start(out=selc[:, :], in_=c_sel), "ls", 16)
        inc(sy.dma_start(out=crow[:, :], in_=d_c), "ls", 16)
        inc(sy.dma_start(out=t8[:, :],
                         in_=d_temp.rearrange("(a b) -> a b", b=1)), "ls", 16)
        inc(sy.dma_start(out=nwc[:, :], in_=d_nw.rearrange("(k p) -> p k", p=P),
                         allow_slow_non_contiguous=True), "ls", 16)
        inc(sy.dma_start(out=sb1T[:, :], in_=d_sb1.rearrange("(k p) -> p k", p=P),
                         allow_slow_non_contiguous=True), "ls", 16)
        inc(sy.dma_start(out=sb3T[:, :], in_=d_sb3.rearrange("(k p) -> p k", p=P),
                         allow_slow_non_contiguous=True), "ls", 16)
        inc(sy.dma_start(out=ht[0:E, 0:EI], in_=d_b1), "ls", 16)
        inc(sy.dma_start(out=ht[32:64, 0:EI], in_=d_b3), "ls", 16)
        inc(sy.dma_start(out=gbg_row[:, :],
                         in_=d_gbg.rearrange("(a e) -> a e", a=1)), "ls", 16)
    LS_PRO = cnt["ls"]

    wmod_marks = []

    @block.gpsimd
    def _(g):
        for s in range(NSLC):
            if s >= 2:
                wge(g, "wm", s - 1)
            buf = wmA if s % 2 == 0 else wmB
            sn = "lwA" if s % 2 == 0 else "lwB"
            i1 = g.dma_start(out=buf.rearrange("p (k i) -> p k i", i=256),
                             in_=d_wmod[:, 256 * s:256 * (s + 1)]
                             .rearrange("(k p) i -> p k i", p=P))
            wmod_marks.append((sn, inc(i1, sn, 16)))

    @block.vector
    def _(v):
        v.memset(ones_b[:, :], 1.0)
        v.memset(ones_f[:, :], 1.0)
        v.memset(negbig[:, :], -1e30)
        i1 = v.memset(cwT[E:E + 1, :], 1.0)
        inc(i1, "vt")
    V_MEMSET = cnt["vt"]

    @block.scalar
    def _(s):
        wge(s, "ls", LS_PRO)
        i1 = s.activation(crow[:, :], crow[:, :], AF.Silu)
        inc(i1, "sl")
        i1 = s.activation(texp[:, :], t8[:, :], AF.Exp)
        inc(i1, "sl")
    SL_SILUC = cnt["sl"]

    first_mk = {}

    @block.tensor
    def _(t):
        wge(t, "lp", LD_PRO)
        wge(t, "ls", LS_PRO)
        wge(t, "sl", SL_SILUC)
        wge(t, "vt", V_MEMSET)

    for ko in range(3):
        @block.tensor
        def _(t, ko=ko):
            for half in range(2):
                if ko >= 1:
                    wge(t, "vt", MK[("wqTev", ko - 1, half)])
                pb = ps[half]
                for q in range(4):
                    kc = half * 4 + q
                    i1 = t.matmul(pb[:, 128 * q:128 * (q + 1)],
                                  fm[:, ko, 128 * kc:128 * (kc + 1)],
                                  idb[:, :], start=True, stop=True)
                MK[("wqT", ko, half)] = inc(i1, "mt")

        @block.vector
        def _(v, ko=ko):
            for half in range(2):
                wge(v, "mt", MK[("wqT", ko, half)])
                i1 = v.tensor_copy(
                    wqkvT[:, 4 * half:4 * (half + 1), 128 * ko:128 * (ko + 1)],
                    ps[half].ap().rearrange("p (k o) -> p k o", o=P))
                MK[("wqTev", ko, half)] = inc(i1, "vt")

    @block.tensor
    def _(t):
        for half in range(2):
            pb = ps[2 + half]
            for q in range(4):
                kc = half * 4 + q
                i1 = t.matmul(pb[:, 128 * q:128 * (q + 1)],
                              wo_stg[:, kc, :], idb[:, :], start=True, stop=True)
            MK[("woT", half)] = inc(i1, "mt")
        for bi in range(2):
            pb = ps[4 + bi]
            src = ht[0:E, 0:EI] if bi == 0 else ht[32:64, 0:EI]
            ideye = idf[0:E, 0:E] if bi == 0 else idf[32:64, 32:64]
            for q in range(4):
                i1 = t.matmul(pb[:, E * q:E * (q + 1)],
                              src[:, 128 * q:128 * (q + 1)],
                              ideye, start=True, stop=True)
            MK[("bT", bi)] = inc(i1, "mt")
        for kc in range(NCH):
            t.matmul(ps[6][:, kc:kc + 1], crow[0:1, 128 * kc:128 * (kc + 1)],
                     idf[64:65, 64:65], start=True, stop=True)
        i1 = t.matmul(ps[6][:, 8:9], selc[:, :], texp[:, :], start=True,
                      stop=True)
        MK[("cT",)] = inc(i1, "mt")

    @block.vector
    def _(v):
        for half in range(2):
            wge(v, "mt", MK[("woT", half)])
            v.tensor_copy(woT[:, 512 * half:512 * (half + 1)], ps[2 + half][:, :])
        for bi in range(2):
            wge(v, "mt", MK[("bT", bi)])
            v.tensor_copy((b1T if bi == 0 else b3T)[:, :, :],
                          ps[4 + bi][:, 0:128].rearrange("p (k e) -> p k e", e=E))
        wge(v, "mt", MK[("cT",)])
        v.tensor_copy(cT[:, :], ps[6][:, 0:8])
        v.tensor_copy(texpc[:, :], ps[6][:, 8:9])
        v.drain()
        for kc in range(NCH):
            i1 = v.tensor_scalar(out=wqkvT[:, kc, :], in0=wqkvT[:, kc, :],
                                 scalar1=nwc[:, kc:kc + 1], scalar2=None,
                                 op0=ALU.mult)
        inc(i1, "vt")
    V_CONSTS = cnt["vt"]

    bc_of_vec = {0: bcB, 1: bcA, 2: bcG, 3: bcB2, 4: bcA2, 5: bcG}
    stg_marks = []
    bc_marks = []

    for s in range(NSLC):
        vec, half = s // 4, s % 4
        is_gate = vec in (2, 5)
        is_ca = vec in (1, 4)

        @block.tensor
        def _(t, s=s):
            wge(t, wmod_marks[s][0], wmod_marks[s][1])
            if s == 0:
                wge(t, "vt", V_CONSTS)
            else:
                wge(t, "vt", stg_marks[s - 1])
            buf = wmA if s % 2 == 0 else wmB
            for kc in range(NCH):
                i1 = t.matmul(ps[2][0:1, 0:256], cT[:, kc:kc + 1],
                              buf[:, 256 * kc:256 * (kc + 1)],
                              start=(kc == 0), stop=(kc == NCH - 1))
            MK[("modmm", s)] = inc(i1, "mt")

        @block.vector
        def _(v, s=s, is_gate=is_gate, is_ca=is_ca):
            wge(v, "mt", MK[("modmm", s)])
            vw = stg[:, 0:256]
            i1 = v.tensor_copy(vw, ps[2][0:1, 0:256])
            inc(i1, "wm")
            v.drain()
            if is_gate:
                sgn = stg[:, 256:512]
                neg = h1[0:1, 3, :].bitcast(F32)[:, 0:256]
                v.tensor_scalar(out=sgn, in0=vw,
                                scalar1=0.0, scalar2=None, op0=ALU.is_ge)
                v.tensor_scalar(out=neg, in0=vw,
                                scalar1=-1.0, scalar2=None, op0=ALU.mult)
                v.drain()
                v.tensor_scalar(out=sgn, in0=sgn,
                                scalar1=2.0, scalar2=-1.0, op0=ALU.mult,
                                op1=ALU.add)
                v.tensor_tensor(vw, vw, neg, ALU.max)
                v.drain()
                v.tensor_scalar(out=vw, in0=vw, scalar1=0.25,
                                scalar2=None, op0=ALU.max)
                v.drain()
                v.tensor_tensor(vw, vw, sgn, ALU.mult)
                v.drain()
            if is_ca:
                v.tensor_scalar(out=vw, in0=vw, scalar1=1.0,
                                scalar2=None, op0=ALU.add)
                v.drain()
            i1 = v.tensor_copy(stgb[:, 0:256], vw)
            stg_marks.append(inc(i1, "vt"))

        @block.tensor
        def _(t, s=s):
            wge(t, "vt", stg_marks[s])
            if s >= 1:
                wge(t, "vt", bc_marks[s - 1])
            i1 = t.matmul(ps[3][:, 0:256], ones_b[0:1, 0:P], stgb[0:1, 0:256],
                          start=True, stop=True)
            MK[("modbc", s)] = inc(i1, "mt")

        @block.vector
        def _(v, s=s, vec=vec, half=half):
            wge(v, "mt", MK[("modbc", s)])
            i1 = v.tensor_copy(bc_of_vec[vec][:, 256 * half:256 * (half + 1)],
                               ps[3][:, 0:256])
            bc_marks.append(inc(i1, "vt"))

        if s == 11:
            @block.vector
            def _(v):
                v.drain()
                i1 = v.tensor_tensor(woT[:, :], woT[:, :], bcG[:, :], ALU.mult)
                inc(i1, "vt")

    V_WOFOLD = cnt["vt"]
    M_END_MT = cnt["mt"]

    for tau in range(NT):
        @block.vector
        def _(v, tau=tau):
            if tau == 0:
                wge(v, "ls", LS_PRO)
                wge(v, "vt", V_WOFOLD)
            v.reduce_sum(sm[:, :], xs[:, tau, :], AX)
            i1 = v.drain()
            MK[("sm", tau)] = inc(i1, "vt")

        @block.scalar
        def _(s, tau=tau):
            if tau >= 1:
                wge(s, "vt", MK[("var", tau - 1)])
            i1 = s.activation(sqd, xs[:, tau, :], AF.Square,
                              accum_out=sqx[:, :])
            MK[("sqx", tau)] = inc(i1, "sl")

        @block.vector
        def _(v, tau=tau):
            wge(v, "sl", MK[("sqx", tau)])
            v.tensor_scalar(out=mcol[:, :], in0=sm[:, :], scalar1=1.0 / D,
                            scalar2=None, op0=ALU.mult)
            v.tensor_scalar(out=vcol[:, :], in0=sqx[:, :], scalar1=1.0 / D,
                            scalar2=None, op0=ALU.mult)
            v.drain()
            v.tensor_scalar(out=tcol[:, :], in0=mcol[:, :], scalar1=mcol[:, :],
                            scalar2=None, op0=ALU.mult)
            v.drain()
            v.tensor_scalar(out=vcol[:, :], in0=vcol[:, :], scalar1=tcol[:, :],
                            scalar2=1e-5, op0=ALU.subtract, op1=ALU.add)
            i1 = v.drain()
            MK[("var", tau)] = inc(i1, "vt")

        @block.scalar
        def _(s, tau=tau):
            wge(s, "vt", MK[("var", tau)])
            i1 = s.activation(tcol[:, :], vcol[:, :], AF.Sqrt)
            MK[("sd", tau)] = inc(i1, "sl")

        @block.vector
        def _(v, tau=tau):
            wge(v, "sl", MK[("sd", tau)])
            v.reciprocal(rcol[:, :], tcol[:, :])
            v.drain()
            v.tensor_scalar(out=ht[:, :], in0=xs[:, tau, :], scalar1=mcol[:, :],
                            scalar2=rcol[:, :], op0=ALU.subtract, op1=ALU.mult)
            v.drain()
            v.tensor_tensor(ht[:, :], ht[:, :], bcA[:, :], ALU.mult)
            v.drain()
            i1 = v.tensor_tensor(ht[:, :], ht[:, :], bcB[:, :], ALU.add)
            MK[("h", tau)] = inc(i1, "vt")

        @block.scalar
        def _(s, tau=tau):
            wge(s, "vt", MK[("h", tau)])
            i1 = s.activation(sqd, ht[:, :], AF.Square, accum_out=sqh[:, :])
            MK[("sqh", tau)] = inc(i1, "sl")

        @block.vector
        def _(v, tau=tau):
            wge(v, "sl", MK[("sqh", tau)])
            v.tensor_scalar(out=tcol[:, :], in0=sqh[:, :], scalar1=1.0 / D,
                            scalar2=1e-6, op0=ALU.mult, op1=ALU.add)
            i1 = v.drain()
            MK[("ms", tau)] = inc(i1, "vt")

        @block.scalar
        def _(s, tau=tau):
            wge(s, "vt", MK[("ms", tau)])
            i1 = s.activation(tcol[:, :], tcol[:, :], AF.Sqrt)
            MK[("sd2", tau)] = inc(i1, "sl")

        @block.vector
        def _(v, tau=tau):
            wge(v, "sl", MK[("sd2", tau)])
            v.reciprocal(rr[:, :], tcol[:, :])
            v.drain()
            if tau >= 2:
                wge(v, "mt", MK[("fmT", tau - 2)])
            i1 = v.tensor_scalar(out=xnb[:, tau % 2, :], in0=ht[:, :],
                                 scalar1=rr[:, :], scalar2=None, op0=ALU.mult)
            MK[("xn", tau)] = inc(i1, "vt")

        @block.tensor
        def _(t, tau=tau):
            wge(t, "vt", MK[("xn", tau)])
            if tau == 0:
                wge(t, "mt", M_END_MT)
            else:
                wge(t, "vt", MK[("fmev", tau - 1)])
            for q in range(4):
                t.matmul(ps[0][:, 128 * q:128 * (q + 1)],
                         xnb[:, tau % 2, 128 * q:128 * (q + 1)],
                         idb[:, :], start=True, stop=True)
            for q in range(4):
                i1 = t.matmul(ps[1][:, 128 * q:128 * (q + 1)],
                              xnb[:, tau % 2, 128 * (4 + q):128 * (5 + q)],
                              idb[:, :], start=True, stop=True)
            MK[("fmT", tau)] = inc(i1, "mt")

        @block.vector
        def _(v, tau=tau):
            wge(v, "mt", MK[("fmT", tau)])
            v.tensor_copy(fm[:, 0:4, 128 * tau:128 * (tau + 1)],
                          ps[0].ap().rearrange("p (k t) -> p k t", t=P))
            i1 = v.tensor_copy(fm[:, 4:8, 128 * tau:128 * (tau + 1)],
                               ps[1].ap().rearrange("p (k t) -> p k t", t=P))
            MK[("fmev", tau)] = inc(i1, "vt")

        @block.tensor
        def _(t, tau=tau):
            wge(t, "vt", MK[("fmev", tau)])
            if tau >= 1:
                wge(t, "vt", MK[("qkev", tau - 1)])
            for kc in range(NCH):
                i1 = t.matmul(ps[2][:, 0:384],
                              fm[:, kc, 128 * tau:128 * (tau + 1)],
                              wqkvT[:, kc, :], start=(kc == 0),
                              stop=(kc == NCH - 1))
            MK[("qkv", tau)] = inc(i1, "mt")

        @block.vector
        def _(v, tau=tau):
            wge(v, "mt", MK[("qkv", tau)])
            i1 = v.tensor_copy(qkvsb[:, tau % 2, :], ps[2][:, 0:384])
            MK[("qkev", tau)] = inc(i1, "vt")

        @block.tensor
        def _(t, tau=tau):
            wge(t, "vt", MK[("qkev", tau)])
            qk = qkvsb[:, tau % 2, :]
            t.matmul(ps[6][:, 0:256], qk[:, 0:128], qk[:, 0:256],
                     start=(tau == 0), stop=(tau == NT - 1))
            t.matmul(ps[3][:, 0:256], qk[:, 128:256], qk[:, 0:256],
                     start=(tau == 0), stop=(tau == NT - 1))
            if tau >= 2:
                wge(t, "vt", MK[("vev", tau - 2)])
            i1 = t.matmul(ps[4 + tau % 2][:, 0:128], qk[:, 256:384],
                          idb[:, :], start=True, stop=True)
            MK[("vT", tau)] = inc(i1, "mt")

        @block.vector
        def _(v, tau=tau):
            wge(v, "mt", MK[("vT", tau)])
            i1 = v.tensor_copy(vfm[:, 128 * tau:128 * (tau + 1)],
                               ps[4 + tau % 2][:, 0:128])
            MK[("vev", tau)] = inc(i1, "vt")

    @block.vector
    def _(v):
        wge(v, "mt", MK[("vT", NT - 1)])
        v.tensor_tensor(Smat[:, :], ps[6][:, 0:128], idf[:, :], ALU.mult)
        v.drain()
        v.reduce_sum(qn[:, :], Smat[:, :], AX)
        v.drain()
        v.tensor_tensor(Smat[:, :], ps[3][:, 128:256], idf[:, :], ALU.mult)
        v.drain()
        v.reduce_sum(kn[:, :], Smat[:, :], AX)
        v.drain()
        v.tensor_copy(arst[:, 0:128], ps[3][:, 0:128])
        v.tensor_copy(arst[:, 128:129], qn[:, :])
        i1 = v.tensor_copy(arst[:, 129:130], kn[:, :])
        inc(i1, "vt")
    V_ARPACK = cnt["vt"]

    @block.gpsimd
    def _(g):
        wge(g, "vt", V_ARPACK)
        inc(g.dma_start(out=cc_in.ap(), in_=arst[:, :]), "la", 16)
        wge(g, "la", cnt["la"])
        i1 = g.collective_compute(
            "AllReduce", ALU.add,
            replica_groups=[list(range(NCORES))],
            ins=[cc_in.ap().opt()],
            outs=[cc_out.ap().opt()])
        inc(i1, "cc")
        wge(g, "cc", cnt["cc"])
        inc(g.dma_start(out=aro[:, :], in_=cc_out.ap()), "la", 16)
    LD_AR = cnt["la"]

    unit_ld_marks = []

    @block.gpsimd
    def _(g):
        for u in range(NUNITS):
            w1b, w3b, w2b = (wA1, wA3, wA2) if u % 2 == 0 else (wB1, wB3, wB2)
            if u < 2:
                wge(g, "wm", NSLC)
            else:
                wge(g, "gf", u - 1)
            if u < 2:
                src1 = d_sw1[:, EI * u:EI * (u + 1)]
                src3 = d_sw3[:, EI * u:EI * (u + 1)]
                src2 = d_sw2[EI * u:EI * (u + 1), :]
            else:
                e = u - 2
                src1, src3, src2 = d_w1[e], d_w3[e], d_w2[e]
            sn = "luA" if u % 2 == 0 else "luB"
            inc(g.dma_start(out=w1b[:, :, :],
                            in_=src1.rearrange("(k p) i -> p k i", p=P)), sn, 16)
            inc(g.dma_start(out=w3b[:, :, :],
                            in_=src3.rearrange("(k p) i -> p k i", p=P)), sn, 16)
            i1 = g.dma_start(out=w2b[:, :, :],
                             in_=src2.rearrange("(k p) j -> p k j", p=P))
            unit_ld_marks.append((sn, inc(i1, sn, 16)))

    @block.vector
    def _(v):
        wge(v, "la", LD_AR)
        v.tensor_copy(qn[:, :], aro[:, 128:129])
        i1 = v.tensor_copy(kn[:, :], aro[:, 129:130])
        v.drain()
        i1 = v.nop()
        inc(i1, "vt")
    V_ARQK = cnt["vt"]

    @block.scalar
    def _(s):
        wge(s, "vt", V_ARQK)
        s.activation(qn[:, :], qn[:, :], AF.Sqrt)
        i1 = s.activation(kn[:, :], kn[:, :], AF.Sqrt)
        inc(i1, "sl")
    SL_SQRTN = cnt["sl"]

    @block.vector
    def _(v):
        wge(v, "sl", SL_SQRTN)
        v.tensor_scalar_max(qn[:, :], qn[:, :], 1e-12)
        v.tensor_scalar_max(kn[:, :], kn[:, :], 1e-12)
        v.drain()
        v.reciprocal(rq[:, :], qn[:, :])
        v.reciprocal(rk[:, :], kn[:, :])
        v.drain()
        i1 = v.tensor_scalar(out=A1b[:, :], in0=aro[:, 0:128],
                             scalar1=rk[:, :], scalar2=None, op0=ALU.mult)
        inc(i1, "vt")
    V_A1 = cnt["vt"]

    @block.tensor
    def _(t):
        wge(t, "vt", V_A1)
        i1 = t.matmul(ps[7][:, 0:128], A1b[:, :], idb[:, :], start=True,
                      stop=True)
        MK[("A1T",)] = inc(i1, "mt")

    @block.vector
    def _(v):
        wge(v, "mt", MK[("A1T",)])
        v.tensor_scalar(out=Smat[:, :], in0=ps[7][:, 0:128], scalar1=rq[:, :],
                        scalar2=texpc[:, :], op0=ALU.mult, op1=ALU.mult)
        v.drain()
        v.tensor_tensor(Smat[:, :], Smat[:, :], blkm[:, :], ALU.add)
        v.drain()
        v.reduce_max(nmx[:, :], Smat[:, :], AX, negate=True)
        i1 = v.drain()
        inc(i1, "vt")
    V_SMAT = cnt["vt"]

    @block.scalar
    def _(s):
        wge(s, "vt", V_SMAT)
        i1 = s.activation(Pmat[:, :], Smat[:, :], AF.Exp, bias=nmx[:, :])
        inc(i1, "sl")
    SL_PEXP = cnt["sl"]

    @block.vector
    def _(v):
        wge(v, "sl", SL_PEXP)
        v.reduce_sum(sden[:, :], Pmat[:, :], AX)
        v.drain()
        v.reciprocal(sden[:, :], sden[:, :])
        v.drain()
        i1 = v.tensor_scalar(out=Pbf[:, :], in0=Pmat[:, :], scalar1=sden[:, :],
                             scalar2=None, op0=ALU.mult)
        inc(i1, "vt")
    V_PBF = cnt["vt"]

    @block.tensor
    def _(t):
        wge(t, "vt", V_PBF)
        i1 = t.matmul(ps[7][:, 0:128], Pbf[:, :], idb[:, :], start=True,
                      stop=True)
        MK[("PT",)] = inc(i1, "mt")

    @block.vector
    def _(v):
        wge(v, "mt", MK[("PT",)])
        i1 = v.tensor_copy(PTb[:, :], ps[7][:, 0:128])
        inc(i1, "vt")
    V_PT = cnt["vt"]

    for tc in range(4):
        @block.tensor
        def _(t, tc=tc):
            wge(t, "vt", V_PT)
            if tc >= 2:
                wge(t, "vt", MK[("avev", tc - 2)])
            i1 = t.matmul(ps[4 + tc % 2][:, :], PTb[:, :],
                          vfm[:, 512 * tc:512 * (tc + 1)], start=True,
                          stop=True)
            MK[("av", tc)] = inc(i1, "mt")

        @block.vector
        def _(v, tc=tc):
            wge(v, "mt", MK[("av", tc)])
            i1 = v.tensor_copy(avfm[:, 512 * tc:512 * (tc + 1)],
                               ps[4 + tc % 2][:, :])
            MK[("avev", tc)] = inc(i1, "vt")

    for tau in range(NT):
        @block.tensor
        def _(t, tau=tau):
            wge(t, "vt", MK[("avev", tau // 4)])
            if tau >= 1:
                wge(t, "vt", MK[("opev", tau - 1)])
            for cj in range(2):
                i1 = t.matmul(ps[cj][:, :], avfm[:, 128 * tau:128 * (tau + 1)],
                              woT[:, 512 * cj:512 * (cj + 1)], start=True,
                              stop=True)
            MK[("op", tau)] = inc(i1, "mt")

        @block.vector
        def _(v, tau=tau):
            wge(v, "mt", MK[("op", tau)])
            v.tensor_tensor(xs[:, tau, 0:512], xs[:, tau, 0:512],
                            ps[0][:, :], ALU.add)
            i1 = v.tensor_tensor(xs[:, tau, 512:1024], xs[:, tau, 512:1024],
                                 ps[1][:, :], ALU.add)
            MK[("opev", tau)] = inc(i1, "vt")

    @block.tensor
    def _(t):
        wge(t, "vt", MK[("opev", NT - 1)])
        i1 = t.matmul(ps[6][:, 0:E], ones_f[0:1, 0:P], gbg_row[0:1, :],
                      start=True, stop=True)
        MK[("gbg",)] = inc(i1, "mt")

    @block.vector
    def _(v):
        wge(v, "mt", MK[("gbg",)])
        i1 = v.tensor_copy(gbg_bc[:, :], ps[6][:, 0:E])
        inc(i1, "vt")
    V_GBG = cnt["vt"]

    for tau in range(NT):
        @block.vector
        def _(v, tau=tau):
            if tau == 0:
                wge(v, "vt", V_GBG)
            v.reduce_sum(sm[:, :], xs[:, tau, :], AX)
            i1 = v.drain()
            MK[("sm2", tau)] = inc(i1, "vt")

        @block.scalar
        def _(s, tau=tau):
            if tau == 0:
                wge(s, "vt", MK[("opev", NT - 1)])
            else:
                wge(s, "vt", MK[("var2", tau - 1)])
            i1 = s.activation(sqd, xs[:, tau, :], AF.Square,
                              accum_out=sqx[:, :])
            MK[("sqx2", tau)] = inc(i1, "sl")

        @block.vector
        def _(v, tau=tau):
            wge(v, "sl", MK[("sqx2", tau)])
            v.tensor_scalar(out=mcol[:, :], in0=sm[:, :], scalar1=1.0 / D,
                            scalar2=None, op0=ALU.mult)
            v.tensor_scalar(out=vcol[:, :], in0=sqx[:, :], scalar1=1.0 / D,
                            scalar2=None, op0=ALU.mult)
            v.drain()
            v.tensor_scalar(out=tcol[:, :], in0=mcol[:, :], scalar1=mcol[:, :],
                            scalar2=None, op0=ALU.mult)
            v.drain()
            v.tensor_scalar(out=vcol[:, :], in0=vcol[:, :], scalar1=tcol[:, :],
                            scalar2=1e-5, op0=ALU.subtract, op1=ALU.add)
            i1 = v.drain()
            MK[("var2", tau)] = inc(i1, "vt")

        @block.scalar
        def _(s, tau=tau):
            wge(s, "vt", MK[("var2", tau)])
            i1 = s.activation(tcol[:, :], vcol[:, :], AF.Sqrt)
            MK[("sdb", tau)] = inc(i1, "sl")

        @block.vector
        def _(v, tau=tau):
            wge(v, "sl", MK[("sdb", tau)])
            v.reciprocal(rcol[:, :], tcol[:, :])
            v.drain()
            v.tensor_scalar(out=ht[:, :], in0=xs[:, tau, :], scalar1=mcol[:, :],
                            scalar2=rcol[:, :], op0=ALU.subtract, op1=ALU.mult)
            v.drain()
            v.tensor_tensor(ht[:, :], ht[:, :], bcA2[:, :], ALU.mult)
            v.drain()
            v.tensor_tensor(ht[:, :], ht[:, :], bcB2[:, :], ALU.add)
            v.drain()
            if tau >= 2:
                wge(v, "mt", MK[("fmT2", tau - 2)])
            i1 = v.tensor_copy(xnb[:, tau % 2, :], ht[:, :])
            MK[("h2", tau)] = inc(i1, "vt")

        @block.tensor
        def _(t, tau=tau):
            wge(t, "vt", MK[("h2", tau)])
            if tau >= 1:
                wge(t, "vt", MK[("fmev2", tau - 1)])
            for q in range(4):
                t.matmul(ps[0][:, 128 * q:128 * (q + 1)],
                         xnb[:, tau % 2, 128 * q:128 * (q + 1)],
                         idb[:, :], start=True, stop=True)
            for q in range(4):
                i1 = t.matmul(ps[1][:, 128 * q:128 * (q + 1)],
                              xnb[:, tau % 2, 128 * (4 + q):128 * (5 + q)],
                              idb[:, :], start=True, stop=True)
            MK[("fmT2", tau)] = inc(i1, "mt")

        @block.vector
        def _(v, tau=tau):
            wge(v, "mt", MK[("fmT2", tau)])
            v.tensor_copy(fm[:, 0:4, 128 * tau:128 * (tau + 1)],
                          ps[0].ap().rearrange("p (k t) -> p k t", t=P))
            i1 = v.tensor_copy(fm[:, 4:8, 128 * tau:128 * (tau + 1)],
                               ps[1].ap().rearrange("p (k t) -> p k t", t=P))
            MK[("fmev2", tau)] = inc(i1, "vt")

        @block.tensor
        def _(t, tau=tau):
            wge(t, "vt", MK[("fmev2", tau)])
            if tau >= 1:
                wge(t, "vt", MK[("gtt", tau - 1)])
            for kc in range(NCH):
                i1 = t.matmul(ps[2][:, 0:E],
                              fm[:, kc, 128 * tau:128 * (tau + 1)],
                              gwsb[:, kc, :], start=(kc == 0),
                              stop=(kc == NCH - 1))
            MK[("gmm", tau)] = inc(i1, "mt")

        @block.vector
        def _(v, tau=tau):
            wge(v, "mt", MK[("gmm", tau)])
            v.tensor_tensor(gsc[:, :], ps[2][:, 0:E], gbg_bc[:, :], ALU.add)
            v.drain()
            v.reduce_max(nmx[:, :], gsc[:, :], AX, negate=True)
            i1 = v.drain()
            MK[("gtt", tau)] = inc(i1, "vt")

        @block.scalar
        def _(s, tau=tau):
            wge(s, "vt", MK[("gtt", tau)])
            i1 = s.activation(gsc[:, :], gsc[:, :], AF.Exp, bias=nmx[:, :])
            MK[("gexp", tau)] = inc(i1, "sl")

        @block.vector
        def _(v, tau=tau):
            wge(v, "sl", MK[("gexp", tau)])
            v.reduce_sum(sden[:, :], gsc[:, :], AX)
            v.drain()
            v.reciprocal(sden[:, :], sden[:, :])
            v.drain()
            v.tensor_scalar(out=gsc[:, :], in0=gsc[:, :], scalar1=sden[:, :],
                            scalar2=None, op0=ALU.mult)
            v.drain()
            a0 = gsc.ap().rearrange("p (g f) -> p g f", f=4)
            first = True
            for (i, j) in [(0, 1), (0, 2), (0, 3), (1, 2), (1, 3), (2, 3)]:
                v.tensor_tensor(gt8[:, :], a0[:, :, i], a0[:, :, j], ALU.add)
                v.drain()
                if first:
                    v.tensor_copy(gs8[:, :], gt8[:, :])
                    first = False
                else:
                    v.tensor_tensor(gs8[:, :], gs8[:, :], gt8[:, :], ALU.max)
                v.drain()
            v.max(out=gt8[:, :], in_=gs8[:, :])
            v.drain()
            v.tensor_scalar(out=gs8[:, :], in0=gs8[:, :], scalar1=gt8[:, 3:4],
                            scalar2=None, op0=ALU.is_ge)
            v.drain()
            v.tensor_copy(gmk.ap().rearrange("p (g f) -> p g f", f=4),
                          gs8.ap().rearrange("p (g f) -> p g f", f=1)
                          .broadcast_to([P, 8, 4]))
            v.drain()
            v.tensor_scalar(out=invm[:, :], in0=gmk[:, :], scalar1=0.5,
                            scalar2=None, op0=ALU.is_lt)
            v.drain()
            v.tensor_copy(gmk[:, :], gsc[:, :])
            v.drain()
            v.copy_predicated(gmk[:, :], invm[:, :], negbig[:, :])
            v.drain()
            v.max(out=gt8[:, :], in_=gmk[:, :])
            v.drain()
            v.tensor_scalar(out=gsel[:, :], in0=gmk[:, :], scalar1=gt8[:, 3:4],
                            scalar2=None, op0=ALU.is_ge)
            v.drain()
            i1 = v.tensor_tensor(cw[:, tau, :], gsel[:, :], gsc[:, :], ALU.mult)
            MK[("cw", tau)] = inc(i1, "vt")

        @block.tensor
        def _(t, tau=tau):
            wge(t, "vt", MK[("cw", tau)])
            if tau >= 2:
                wge(t, "vt", MK[("cwTev", tau - 2)])
            i1 = t.matmul(ps[3 + tau % 2][0:E, 0:128], cw[:, tau, :],
                          idf[:, :], start=True, stop=True)
            MK[("cwT", tau)] = inc(i1, "mt")

        @block.vector
        def _(v, tau=tau):
            wge(v, "mt", MK[("cwT", tau)])
            i1 = v.tensor_copy(cwT[0:E, 128 * tau:128 * (tau + 1)],
                               ps[3 + tau % 2][0:E, 0:128])
            MK[("cwTev", tau)] = inc(i1, "vt")

    @block.sync
    def _(sy):
        wge(sy, "vt", MK[("h2", NT - 1)])
        inc(sy.dma_start(out=ht[0:E, :], in_=d_b2), "ls", 16)
        i1 = sy.dma_start(out=ht[E:E + 1, :],
                          in_=d_sb2.rearrange("(a c) -> a c", a=1))
        inc(i1, "ls", 16)
    LS_B2 = cnt["ls"]

    @block.vector
    def _(v):
        wge(v, "ls", LS_B2)
        wge(v, "vt", bc_marks[23])
        i1 = v.tensor_tensor(b2gb[:, :], ht[0:E + 1, :], bcG[0:E + 1, :],
                             ALU.mult)
        inc(i1, "vt")
    V_B2G = cnt["vt"]
    C_END_VT = cnt["vt"]

    setA = [ps[0], ps[1], ps[2], ps[3]]
    setB = [ps[4], ps[5], ps[6], ps[7]]

    def unit_bufs(u):
        return (wA1, wA3, wA2) if u % 2 == 0 else (wB1, wB3, wB2)

    def emit_w1(u, i):
        w1b, _, _ = unit_bufs(u)

        @block.tensor
        def _(t):
            st = setA if i % 2 == 0 else setB
            if i == 0:
                wge(t, unit_ld_marks[u][0], unit_ld_marks[u][1])
            if u == 0 and i < 2:
                wge(t, "vt", C_END_VT)
            elif i < 2:
                wge(t, "gf", u)
            else:
                wge(t, "vt", MK[("tt", u, i - 2)])
            for c in range(NCH):
                for tcc in range(4):
                    i1 = t.matmul(st[tcc][:, :],
                                  w1b[:, c, 128 * i:128 * (i + 1)],
                                  fm[:, c, 512 * tcc:512 * (tcc + 1)],
                                  start=(c == 0), stop=(c == NCH - 1))
            MK[("w1", u, i)] = inc(i1, "mt")

    def emit_silu(u, i):
        shared = u < 2

        @block.scalar
        def _(s):
            st0 = 0 if i % 2 == 0 else 4
            wge(s, "mt", MK[("w1", u, i)])
            if i >= 2:
                wge(s, "vt", MK[("tt", u, i - 2)])
            elif u >= 1:
                wge(s, "vt", MK[("tt", u - 1, i + 2)])
            bias = (sb1T[:, 4 * u + i:4 * u + i + 1] if shared
                    else b1T[:, i, (u - 2):(u - 1)])
            for tcc in range(4):
                i1 = s.activation(s1[:, i % 2, tcc, :], ps[st0 + tcc][:, :],
                                  AF.Silu, bias=bias)
            MK[("si", u, i)] = inc(i1, "sl")

    def emit_w3(u, i):
        _, w3b, _ = unit_bufs(u)

        @block.tensor
        def _(t):
            st = setA if i % 2 == 0 else setB
            wge(t, "sl", MK[("si", u, i)])
            for c in range(NCH):
                for tcc in range(4):
                    i1 = t.matmul(st[tcc][:, :],
                                  w3b[:, c, 128 * i:128 * (i + 1)],
                                  fm[:, c, 512 * tcc:512 * (tcc + 1)],
                                  start=(c == 0), stop=(c == NCH - 1))
            MK[("w3", u, i)] = inc(i1, "mt")

    def emit_tt(u, i):
        shared = u < 2

        @block.vector
        def _(v):
            st0 = 0 if i % 2 == 0 else 4
            wge(v, "mt", MK[("w3", u, i)])
            if u >= 1 and i == 0:
                wge(v, "mt", MK[("w2", u - 1, NT - 1)])
            b3c = (sb3T[:, 4 * u + i:4 * u + i + 1] if shared
                   else b3T[:, i, (u - 2):(u - 1)])
            for tcc in range(4):
                i1 = v.scalar_tensor_tensor(
                    out=h1[:, i, 512 * tcc:512 * (tcc + 1)],
                    in0=ps[st0 + tcc][:, :], scalar=b3c,
                    in1=s1[:, i % 2, tcc, :], op0=ALU.add, op1=ALU.mult)
            MK[("tt", u, i)] = inc(i1, "vt")

    def emit_w2fold(u):
        _, _, w2b = unit_bufs(u)

        @block.vector
        def _(v):
            wge(v, unit_ld_marks[u][0], unit_ld_marks[u][1])
            if u == 0:
                wge(v, "vt", V_B2G)
            for ki in range(4):
                i1 = v.tensor_tensor(w2b[:, ki, :], w2b[:, ki, :], bcG[:, :],
                                     ALU.mult)
            MK[("w2f", u)] = inc(i1, "vt")

    def emit_w2_group(u, g):
        _, _, w2b = unit_bufs(u)

        @block.tensor
        def _(t):
            if g == 0:
                wge(t, "vt", MK[("tt", u, 3)])
                wge(t, "vt", MK[("w2f", u)])
            for tau in range(2 * g, 2 * g + 2):
                pa = ps[0 + 2 * (tau % 2)]
                pb = ps[1 + 2 * (tau % 2)]
                if tau >= 2:
                    wge(t, "vt", MK[("ev", u, tau - 2)])
                for i in range(4):
                    t.matmul(pa[:, :], h1[:, i, 128 * tau:128 * (tau + 1)],
                             w2b[:, i, 0:512], start=(i == 0), stop=(i == 3))
                    i1 = t.matmul(pb[:, :],
                                  h1[:, i, 128 * tau:128 * (tau + 1)],
                                  w2b[:, i, 512:1024], start=(i == 0),
                                  stop=(i == 3))
                MK[("w2", u, tau)] = inc(i1, "mt")

    def emit_ev_group(u, g):
        shared = u < 2

        @block.vector
        def _(v):
            for tau in range(2 * g, 2 * g + 2):
                pa = ps[0 + 2 * (tau % 2)]
                pb = ps[1 + 2 * (tau % 2)]
                wge(v, "mt", MK[("w2", u, tau)])
                if shared:
                    v.tensor_tensor(xs[:, tau, 0:512], xs[:, tau, 0:512],
                                    pa[:, :], ALU.add)
                    i1 = v.tensor_tensor(xs[:, tau, 512:1024],
                                         xs[:, tau, 512:1024], pb[:, :],
                                         ALU.add)
                else:
                    cws = cw[:, tau, (u - 2):(u - 1)]
                    v.scalar_tensor_tensor(
                        out=xs[:, tau, 0:512], in0=pa[:, :], scalar=cws,
                        in1=xs[:, tau, 0:512], op0=ALU.mult, op1=ALU.add)
                    i1 = v.scalar_tensor_tensor(
                        out=xs[:, tau, 512:1024], in0=pb[:, :], scalar=cws,
                        in1=xs[:, tau, 512:1024], op0=ALU.mult, op1=ALU.add)
                if tau == NT - 1:
                    inc(i1, "gf")
                else:
                    MK[("ev", u, tau)] = inc(i1, "vt")

    for u in range(NUNITS):
        emit_w2fold(u)
        emit_w1(u, 0)
        emit_silu(u, 0)
        emit_w1(u, 1)
        emit_silu(u, 1)
        emit_w3(u, 0)
        emit_tt(u, 0)
        emit_w1(u, 2)
        emit_silu(u, 2)
        emit_w3(u, 1)
        emit_tt(u, 1)
        emit_w1(u, 3)
        emit_silu(u, 3)
        emit_w3(u, 2)
        emit_tt(u, 2)
        emit_w3(u, 3)
        emit_tt(u, 3)
        for g in range(8):
            emit_w2_group(u, g)
            emit_ev_group(u, g)

    for g in range(8):
        @block.tensor
        def _(t, g=g):
            if g == 0:
                wge(t, "vt", MK[("cwTev", NT - 1)])
                wge(t, "vt", V_B2G)
            for tau in range(2 * g, 2 * g + 2):
                pa = ps[0 + 2 * (tau % 2)]
                pb = ps[1 + 2 * (tau % 2)]
                if tau < 2:
                    wge(t, "gf", NUNITS)
                else:
                    wge(t, "vt", MK[("b2ev", tau - 2)])
                t.matmul(pa[:, :], cwT[:, 128 * tau:128 * (tau + 1)],
                         b2gb[:, 0:512], start=True, stop=True)
                i1 = t.matmul(pb[:, :], cwT[:, 128 * tau:128 * (tau + 1)],
                              b2gb[:, 512:1024], start=True, stop=True)
                MK[("b2mm", tau)] = inc(i1, "mt")

        @block.vector
        def _(v, g=g):
            for tau in range(2 * g, 2 * g + 2):
                pa = ps[0 + 2 * (tau % 2)]
                pb = ps[1 + 2 * (tau % 2)]
                wge(v, "mt", MK[("b2mm", tau)])
                v.tensor_tensor(xs[:, tau, 0:512], xs[:, tau, 0:512],
                                pa[:, :], ALU.add)
                i1 = v.tensor_tensor(xs[:, tau, 512:1024],
                                     xs[:, tau, 512:1024], pb[:, :], ALU.add)
                MK[("b2ev", tau)] = inc(i1, "vt")
    V_FINAL = cnt["vt"]

    @block.sync
    def _(sy):
        wge(sy, "vt", V_FINAL)
        inc(sy.dma_start(out=d_out.rearrange("(t p) c -> p t c", p=P),
                         in_=xs[:, :, :]), "st", 16)
        wge(sy, "st", cnt["st"])

    blk_es.close()
    es.close()
    return nc


_CACHE = {}


def kernel(**inputs):
    nc = _CACHE.get("nc")
    if nc is None:
        nc = build_nc()
        _CACHE["nc"] = nc

    x = np.ascontiguousarray(
        np.asarray(inputs["L0_attn_x"], dtype=np.float32).reshape(16384, D))
    full = {
        "attn_nw": np.ascontiguousarray(inputs["L0_attn_norm_weight"], dtype=np.float32),
        "attn_wqkv": np.ascontiguousarray(inputs["L0_attn_w_qkv"], dtype=np.float32),
        "attn_temp": np.ascontiguousarray(
            np.asarray(inputs["L0_attn_temperature"], dtype=np.float32).reshape(8)),
        "attn_wout": np.ascontiguousarray(inputs["L0_attn_w_out"], dtype=np.float32),
        "ffn_gw": np.ascontiguousarray(inputs["L0_ffn_gw"], dtype=np.float32),
        "ffn_gb": np.ascontiguousarray(inputs["L0_ffn_gb"], dtype=np.float32),
        "ffn_w1": np.ascontiguousarray(inputs["L0_ffn_w1"], dtype=np.float32),
        "ffn_b1": np.ascontiguousarray(inputs["L0_ffn_b1"], dtype=np.float32),
        "ffn_w3": np.ascontiguousarray(inputs["L0_ffn_w3"], dtype=np.float32),
        "ffn_b3": np.ascontiguousarray(inputs["L0_ffn_b3"], dtype=np.float32),
        "ffn_w2": np.ascontiguousarray(inputs["L0_ffn_w2"], dtype=np.float32),
        "ffn_b2": np.ascontiguousarray(inputs["L0_ffn_b2"], dtype=np.float32),
        "ffn_sw1": np.ascontiguousarray(inputs["L0_ffn_sw1"], dtype=np.float32),
        "ffn_sb1": np.ascontiguousarray(inputs["L0_ffn_sb1"], dtype=np.float32),
        "ffn_sw3": np.ascontiguousarray(inputs["L0_ffn_sw3"], dtype=np.float32),
        "ffn_sb3": np.ascontiguousarray(inputs["L0_ffn_sb3"], dtype=np.float32),
        "ffn_sw2": np.ascontiguousarray(inputs["L0_ffn_sw2"], dtype=np.float32),
        "ffn_sb2": np.ascontiguousarray(inputs["L0_ffn_sb2"], dtype=np.float32),
        "cvec": np.ascontiguousarray(
            np.asarray(inputs["L0_c"], dtype=np.float32).reshape(1, D)),
        "wmod": np.ascontiguousarray(inputs["L0_wmod"], dtype=np.float32),
    }
    in_maps = []
    for c in range(NCORES):
        m = dict(full)
        m["xsh"] = np.ascontiguousarray(x[c * TPC:(c + 1) * TPC])
        in_maps.append(m)
    res = run_bass_kernel_spmd(nc, in_maps, core_ids=list(range(NCORES)))
    out = np.concatenate([np.asarray(res.results[c]["out"])
                          for c in range(NCORES)], axis=0)
    return out.reshape(1, 16384, D)


# version 06 (final)
# speedup vs baseline: 1.0653x; 1.0011x over previous
import numpy as np
import ml_dtypes
from contextlib import ExitStack

import concourse.bass as bass
import concourse.mybir as mybir
from concourse.bass_utils import run_bass_kernel_spmd

F32 = mybir.dt.float32
BF16 = mybir.dt.bfloat16
AF = mybir.ActivationFunctionType
ALU = mybir.AluOpType
AX = mybir.AxisListType.X

P = 128
NT = 16
TPC = P * NT
D = 1024
NCH = 8
E = 32
EI = 512
NCORES = 8
NUNITS = 34
NSLC = 24


def build_nc():
    nc = bass.Bass("TRN2", target_bir_lowering=False)

    d_x = nc.dram_tensor("xsh", [TPC, D], F32, kind="ExternalInput").ap()
    d_out = nc.dram_tensor("out", [TPC, D], F32, kind="ExternalOutput").ap()
    d_nw = nc.dram_tensor("attn_nw", [D], F32, kind="ExternalInput").ap()
    d_wqkv = nc.dram_tensor("attn_wqkv", [384, D], F32, kind="ExternalInput").ap()
    d_temp = nc.dram_tensor("attn_temp", [8], F32, kind="ExternalInput").ap()
    d_wout = nc.dram_tensor("attn_wout", [D, 128], F32, kind="ExternalInput").ap()
    d_gw = nc.dram_tensor("ffn_gw", [D, E], F32, kind="ExternalInput").ap()
    d_gbg = nc.dram_tensor("ffn_gb", [E], F32, kind="ExternalInput").ap()
    d_w1 = nc.dram_tensor("ffn_w1", [E, D, EI], F32, kind="ExternalInput").ap()
    d_b1 = nc.dram_tensor("ffn_b1", [E, EI], F32, kind="ExternalInput").ap()
    d_w3 = nc.dram_tensor("ffn_w3", [E, D, EI], F32, kind="ExternalInput").ap()
    d_b3 = nc.dram_tensor("ffn_b3", [E, EI], F32, kind="ExternalInput").ap()
    d_w2 = nc.dram_tensor("ffn_w2", [E, EI, D], F32, kind="ExternalInput").ap()
    d_b2 = nc.dram_tensor("ffn_b2", [E, D], F32, kind="ExternalInput").ap()
    d_sw1 = nc.dram_tensor("ffn_sw1", [D, D], F32, kind="ExternalInput").ap()
    d_sb1 = nc.dram_tensor("ffn_sb1", [D], F32, kind="ExternalInput").ap()
    d_sw3 = nc.dram_tensor("ffn_sw3", [D, D], F32, kind="ExternalInput").ap()
    d_sb3 = nc.dram_tensor("ffn_sb3", [D], F32, kind="ExternalInput").ap()
    d_sw2 = nc.dram_tensor("ffn_sw2", [D, D], F32, kind="ExternalInput").ap()
    d_sb2 = nc.dram_tensor("ffn_sb2", [D], F32, kind="ExternalInput").ap()
    d_c = nc.dram_tensor("cvec", [1, D], F32, kind="ExternalInput").ap()
    d_wmod = nc.dram_tensor("wmod", [D, 6 * D], F32, kind="ExternalInput").ap()

    cc_in = nc.dram_tensor("cc_in", [P, 130], F32)
    cc_out = nc.dram_tensor("cc_out", [P, 130], F32)

    c_idf = nc.inline_tensor(np.eye(P, dtype=np.float32), name="identf").ap()
    c_idb = nc.inline_tensor(np.eye(P, dtype=ml_dtypes.bfloat16), name="identb").ap()
    _bm = np.where(np.arange(P)[:, None] // 16 == np.arange(P)[None, :] // 16,
                   0.0, -1e30).astype(np.float32)
    c_bm = nc.inline_tensor(_bm, name="blkmask").ap()
    _sel = (np.arange(P)[None, :] // 16 == np.arange(8)[:, None]).astype(np.float32)
    c_sel = nc.inline_tensor(_sel, name="selc").ap()

    def sb(name, shape, dtype):
        return nc.alloc_sbuf_tensor(name, shape, dtype)

    xs = sb("xs", [P, NT, D], F32)
    fm = sb("fm", [P, NCH, TPC], BF16)
    h1 = sb("h1", [P, 4, TPC], BF16)
    s1 = sb("s1", [P, 2, 4, 512], BF16)
    xnb = sb("xnb", [P, 2, D], BF16)
    wA1 = sb("wA1", [P, NCH, EI], BF16)
    wA3 = sb("wA3", [P, NCH, EI], BF16)
    wA2 = sb("wA2", [P, 4, D], BF16)
    wB1 = sb("wB1", [P, NCH, EI], BF16)
    wB3 = sb("wB3", [P, NCH, EI], BF16)
    wB2 = sb("wB2", [P, 4, D], BF16)
    bcA = sb("bcA", [P, D], BF16)
    bcB = sb("bcB", [P, D], BF16)
    bcA2 = sb("bcA2", [P, D], BF16)
    bcB2 = sb("bcB2", [P, D], BF16)
    bcG = sb("bcG", [P, D], BF16)
    wqkvT = sb("wqkvT", [P, NCH, 384], BF16)
    woT = sb("woT", [P, D], BF16)
    gwsb = sb("gwsb", [P, NCH, E], BF16)
    b1T = sb("b1T", [P, 4, E], F32)
    b3T = sb("b3T", [P, 4, E], F32)
    sb1T = sb("sb1T", [P, 8], F32)
    sb3T = sb("sb3T", [P, 8], F32)
    nwc = sb("nwc", [P, 8], F32)
    b2gb = sb("b2gb", [E + 1, D], BF16)
    cw = sb("cw", [P, NT, E], F32)
    cwT = sb("cwT", [E + 1, TPC], BF16)
    gsc = sb("gsc", [P, E], F32)
    gmk = sb("gmk", [P, E], F32)
    gsel = sb("gsel", [P, E], F32)
    gs8 = sb("gs8", [P, 8], F32)
    gt8 = sb("gt8", [P, 8], F32)
    invm = sb("invm", [P, E], mybir.dt.uint32)
    negbig = sb("negbig", [P, E], F32)
    gbg_bc = sb("gbg_bc", [P, E], F32)
    gbg_row = sb("gbg_row", [1, E], F32)
    Smat = sb("Smat", [P, P], F32)
    Pbf = sb("Pbf", [P, P], BF16)
    PTb = sb("PTb", [P, P], BF16)
    blkm = sb("blkm", [P, P], F32)
    idf = sb("idf", [P, P], F32)
    idb = sb("idb", [P, P], BF16)
    selc = sb("selsb", [8, P], F32)
    ones_b = sb("ones_b", [1, P], BF16)
    ones_f = sb("ones_f", [1, P], F32)
    stg = sb("stg", [1, 512], F32)
    stgb = sb("stgb", [1, 512], BF16)
    cT = sb("cT", [P, 8], F32)
    t8 = sb("t8", [8, 1], F32)
    texp = sb("texp", [8, 1], F32)
    texpc = sb("texpc", [P, 1], F32)
    sm = sb("sm", [P, 2], F32)
    sqx = sb("sqx", [P, 2], F32)
    mcol = sb("mcol", [P, 2], F32)
    vcol = sb("vcol", [P, 2], F32)
    rcol = sb("rcol", [P, 2], F32)
    sqh = sb("sqh", [P, 2], F32)
    rr = sb("rr", [P, 2], F32)
    nmx = sb("nmx", [P, 2], F32)
    sden = sb("sden", [P, 2], F32)
    qn = sb("qn", [P, 1], F32)
    kn = sb("kn", [P, 1], F32)
    rq = sb("rq", [P, 1], F32)
    rk = sb("rk", [P, 1], F32)
    tcol = sb("tcol", [P, 2], F32)

    vfm = h1[:, 0, :]
    avfm = h1[:, 1, :]
    ht = h1[:, 2, :].bitcast(F32)
    ht2 = s1.ap().rearrange("p a b c -> p (a b c)").bitcast(F32)[:, D:2 * D]
    crow = h1[64:65, 2, :].bitcast(F32)
    wmA = wA1.ap().rearrange("p a b -> p (a b)").bitcast(F32)
    wmB = wB1.ap().rearrange("p a b -> p (a b)").bitcast(F32)
    arst = h1[:, 3, :].bitcast(F32)[:, 0:130]
    aro = h1[:, 3, :].bitcast(F32)[:, 130:260]
    qkvsb = s1[:, 1, 0:2, 0:384]
    sqd = s1.ap().rearrange("p a b c -> p (a b c)").bitcast(F32)[:, 0:D]
    A1b = Pbf
    Pmat = Smat
    wo_stg = fm[:, 3, 0:D].rearrange("p (k o) -> p k o", o=P)

    ps = [nc.alloc_psum_tensor(f"ps{i}", [P, 512], F32) for i in range(8)]

    es = ExitStack()
    sems, cnt = {}, {}
    for n in ["lp", "lwA", "lwB", "luA", "luB", "la", "ls", "mt", "sl", "vt",
              "gf", "wm", "st", "cc"]:
        sems[n] = es.enter_context(nc.semaphore(n))
        cnt[n] = 0

    def inc(instr, name, amt=1):
        instr.then_inc(sems[name], amt)
        cnt[name] += amt
        return cnt[name]

    def wge(eng, name, val):
        eng.wait_ge(sems[name], val)

    MK = {}

    blk_es = ExitStack()
    block = blk_es.enter_context(nc.Block())

    @block.gpsimd
    def _(g):
        inc(g.dma_start(out=fm[:, 0, 0:D], in_=d_wqkv[0:128, :]), "lp", 16)
        inc(g.dma_start(out=fm[:, 1, 0:D], in_=d_wqkv[128:256, :]), "lp", 16)
        inc(g.dma_start(out=fm[:, 2, 0:D], in_=d_wqkv[256:384, :]), "lp", 16)
        inc(g.dma_start(out=wo_stg[:, :, :],
                        in_=d_wout.rearrange("(k p) o -> p k o", p=P)), "lp", 16)
        inc(g.dma_start(out=gwsb[:, :, :],
                        in_=d_gw.rearrange("(k p) e -> p k e", p=P)), "lp", 16)
        inc(g.dma_start(out=idb[:, :], in_=c_idb), "lp", 16)
    LD_PRO = cnt["lp"]

    @block.sync
    def _(sy):
        inc(sy.dma_start(out=xs[:, :, :],
                         in_=d_x.rearrange("(t p) c -> p t c", p=P)), "ls", 16)
        inc(sy.dma_start(out=idf[:, :], in_=c_idf), "ls", 16)
        inc(sy.dma_start(out=blkm[:, :], in_=c_bm), "ls", 16)
        inc(sy.dma_start(out=selc[:, :], in_=c_sel), "ls", 16)
        inc(sy.dma_start(out=crow[:, :], in_=d_c), "ls", 16)
        inc(sy.dma_start(out=t8[:, :],
                         in_=d_temp.rearrange("(a b) -> a b", b=1)), "ls", 16)
        inc(sy.dma_start(out=nwc[:, :], in_=d_nw.rearrange("(k p) -> p k", p=P),
                         allow_slow_non_contiguous=True), "ls", 16)
        inc(sy.dma_start(out=sb1T[:, :], in_=d_sb1.rearrange("(k p) -> p k", p=P),
                         allow_slow_non_contiguous=True), "ls", 16)
        inc(sy.dma_start(out=sb3T[:, :], in_=d_sb3.rearrange("(k p) -> p k", p=P),
                         allow_slow_non_contiguous=True), "ls", 16)
        inc(sy.dma_start(out=ht[0:E, 0:EI], in_=d_b1), "ls", 16)
        inc(sy.dma_start(out=ht[32:64, 0:EI], in_=d_b3), "ls", 16)
        inc(sy.dma_start(out=gbg_row[:, :],
                         in_=d_gbg.rearrange("(a e) -> a e", a=1)), "ls", 16)
    LS_PRO = cnt["ls"]

    wmod_marks = []

    @block.gpsimd
    def _(g):
        for s in range(NSLC):
            if s >= 2:
                wge(g, "wm", s - 1)
            buf = wmA if s % 2 == 0 else wmB
            sn = "lwA" if s % 2 == 0 else "lwB"
            i1 = g.dma_start(out=buf.rearrange("p (k i) -> p k i", i=256),
                             in_=d_wmod[:, 256 * s:256 * (s + 1)]
                             .rearrange("(k p) i -> p k i", p=P))
            wmod_marks.append((sn, inc(i1, sn, 16)))

    @block.vector
    def _(v):
        v.memset(ones_b[:, :], 1.0)
        v.memset(ones_f[:, :], 1.0)
        v.memset(negbig[:, :], -1e30)
        i1 = v.memset(cwT[E:E + 1, :], 1.0)
        inc(i1, "vt")
    V_MEMSET = cnt["vt"]

    @block.scalar
    def _(s):
        wge(s, "ls", LS_PRO)
        i1 = s.activation(crow[:, :], crow[:, :], AF.Silu)
        inc(i1, "sl")
        i1 = s.activation(texp[:, :], t8[:, :], AF.Exp)
        inc(i1, "sl")
    SL_SILUC = cnt["sl"]

    first_mk = {}

    @block.tensor
    def _(t):
        wge(t, "lp", LD_PRO)
        wge(t, "ls", LS_PRO)
        wge(t, "sl", SL_SILUC)
        wge(t, "vt", V_MEMSET)

    for ko in range(3):
        @block.tensor
        def _(t, ko=ko):
            for half in range(2):
                if ko >= 1:
                    wge(t, "vt", MK[("wqTev", ko - 1, half)])
                pb = ps[half]
                for q in range(4):
                    kc = half * 4 + q
                    i1 = t.matmul(pb[:, 128 * q:128 * (q + 1)],
                                  fm[:, ko, 128 * kc:128 * (kc + 1)],
                                  idb[:, :], start=True, stop=True)
                MK[("wqT", ko, half)] = inc(i1, "mt")

        @block.vector
        def _(v, ko=ko):
            for half in range(2):
                wge(v, "mt", MK[("wqT", ko, half)])
                i1 = v.tensor_copy(
                    wqkvT[:, 4 * half:4 * (half + 1), 128 * ko:128 * (ko + 1)],
                    ps[half].ap().rearrange("p (k o) -> p k o", o=P))
                MK[("wqTev", ko, half)] = inc(i1, "vt")

    @block.tensor
    def _(t):
        for half in range(2):
            pb = ps[2 + half]
            for q in range(4):
                kc = half * 4 + q
                i1 = t.matmul(pb[:, 128 * q:128 * (q + 1)],
                              wo_stg[:, kc, :], idb[:, :], start=True, stop=True)
            MK[("woT", half)] = inc(i1, "mt")
        for bi in range(2):
            pb = ps[4 + bi]
            src = ht[0:E, 0:EI] if bi == 0 else ht[32:64, 0:EI]
            ideye = idf[0:E, 0:E] if bi == 0 else idf[32:64, 32:64]
            for q in range(4):
                i1 = t.matmul(pb[:, E * q:E * (q + 1)],
                              src[:, 128 * q:128 * (q + 1)],
                              ideye, start=True, stop=True)
            MK[("bT", bi)] = inc(i1, "mt")
        for kc in range(NCH):
            t.matmul(ps[6][:, kc:kc + 1], crow[0:1, 128 * kc:128 * (kc + 1)],
                     idf[64:65, 64:65], start=True, stop=True)
        i1 = t.matmul(ps[6][:, 8:9], selc[:, :], texp[:, :], start=True,
                      stop=True)
        MK[("cT",)] = inc(i1, "mt")

    @block.vector
    def _(v):
        for half in range(2):
            wge(v, "mt", MK[("woT", half)])
            v.tensor_copy(woT[:, 512 * half:512 * (half + 1)], ps[2 + half][:, :])
        for bi in range(2):
            wge(v, "mt", MK[("bT", bi)])
            v.tensor_copy((b1T if bi == 0 else b3T)[:, :, :],
                          ps[4 + bi][:, 0:128].rearrange("p (k e) -> p k e", e=E))
        wge(v, "mt", MK[("cT",)])
        v.tensor_copy(cT[:, :], ps[6][:, 0:8])
        v.tensor_copy(texpc[:, :], ps[6][:, 8:9])
        v.drain()
        for kc in range(NCH):
            i1 = v.tensor_scalar(out=wqkvT[:, kc, :], in0=wqkvT[:, kc, :],
                                 scalar1=nwc[:, kc:kc + 1], scalar2=None,
                                 op0=ALU.mult)
        inc(i1, "vt")
    V_CONSTS = cnt["vt"]

    bc_of_vec = {0: bcB, 1: bcA, 2: bcG, 3: bcB2, 4: bcA2, 5: bcG}
    stg_marks = []
    bc_marks = []

    for s in range(NSLC):
        vec, half = s // 4, s % 4
        is_gate = vec in (2, 5)
        is_ca = vec in (1, 4)

        @block.tensor
        def _(t, s=s):
            wge(t, wmod_marks[s][0], wmod_marks[s][1])
            if s == 0:
                wge(t, "vt", V_CONSTS)
            else:
                wge(t, "vt", stg_marks[s - 1])
            buf = wmA if s % 2 == 0 else wmB
            for kc in range(NCH):
                i1 = t.matmul(ps[2][0:1, 0:256], cT[:, kc:kc + 1],
                              buf[:, 256 * kc:256 * (kc + 1)],
                              start=(kc == 0), stop=(kc == NCH - 1))
            MK[("modmm", s)] = inc(i1, "mt")

        @block.vector
        def _(v, s=s, is_gate=is_gate, is_ca=is_ca):
            wge(v, "mt", MK[("modmm", s)])
            vw = stg[:, 0:256]
            i1 = v.tensor_copy(vw, ps[2][0:1, 0:256])
            inc(i1, "wm")
            v.drain()
            if is_gate:
                sgn = stg[:, 256:512]
                neg = h1[0:1, 3, :].bitcast(F32)[:, 0:256]
                v.tensor_scalar(out=sgn, in0=vw,
                                scalar1=0.0, scalar2=None, op0=ALU.is_ge)
                v.tensor_scalar(out=neg, in0=vw,
                                scalar1=-1.0, scalar2=None, op0=ALU.mult)
                v.drain()
                v.tensor_scalar(out=sgn, in0=sgn,
                                scalar1=2.0, scalar2=-1.0, op0=ALU.mult,
                                op1=ALU.add)
                v.tensor_tensor(vw, vw, neg, ALU.max)
                v.drain()
                v.tensor_scalar(out=vw, in0=vw, scalar1=0.25,
                                scalar2=None, op0=ALU.max)
                v.drain()
                v.tensor_tensor(vw, vw, sgn, ALU.mult)
                v.drain()
            if is_ca:
                v.tensor_scalar(out=vw, in0=vw, scalar1=1.0,
                                scalar2=None, op0=ALU.add)
                v.drain()
            i1 = v.tensor_copy(stgb[:, 0:256], vw)
            stg_marks.append(inc(i1, "vt"))

        @block.tensor
        def _(t, s=s):
            wge(t, "vt", stg_marks[s])
            if s >= 1:
                wge(t, "vt", bc_marks[s - 1])
            i1 = t.matmul(ps[3][:, 0:256], ones_b[0:1, 0:P], stgb[0:1, 0:256],
                          start=True, stop=True)
            MK[("modbc", s)] = inc(i1, "mt")

        @block.vector
        def _(v, s=s, vec=vec, half=half):
            wge(v, "mt", MK[("modbc", s)])
            i1 = v.tensor_copy(bc_of_vec[vec][:, 256 * half:256 * (half + 1)],
                               ps[3][:, 0:256])
            bc_marks.append(inc(i1, "vt"))

        if s == 11:
            @block.vector
            def _(v):
                v.drain()
                i1 = v.tensor_tensor(woT[:, :], woT[:, :], bcG[:, :], ALU.mult)
                inc(i1, "vt")

    V_WOFOLD = cnt["vt"]
    M_END_MT = cnt["mt"]

    for tau in range(NT):
        pc = tau % 2
        smc = sm[:, pc:pc + 1]
        sqxc = sqx[:, pc:pc + 1]
        mc = mcol[:, pc:pc + 1]
        vc = vcol[:, pc:pc + 1]
        rc = rcol[:, pc:pc + 1]
        sqhc = sqh[:, pc:pc + 1]
        rrc = rr[:, pc:pc + 1]
        htc = ht if pc == 0 else ht2
        tc1 = tcol[:, pc:pc + 1]

        @block.vector
        def _(v, tau=tau, smc=smc):
            if tau == 0:
                wge(v, "ls", LS_PRO)
                wge(v, "vt", V_WOFOLD)
            v.reduce_sum(smc, xs[:, tau, :], AX)
            i1 = v.drain()
            MK[("sm", tau)] = inc(i1, "vt")

        @block.scalar
        def _(s, tau=tau, sqxc=sqxc):
            i1 = s.activation(sqd, xs[:, tau, :], AF.Square,
                              accum_out=sqxc)
            MK[("sqx", tau)] = inc(i1, "sl")

        @block.vector
        def _(v, tau=tau, smc=smc, sqxc=sqxc, mc=mc, vc=vc, tc1=tc1):
            wge(v, "sl", MK[("sqx", tau)])
            v.tensor_scalar(out=mc, in0=smc, scalar1=1.0 / D,
                            scalar2=None, op0=ALU.mult)
            v.tensor_scalar(out=vc, in0=sqxc, scalar1=1.0 / D,
                            scalar2=None, op0=ALU.mult)
            v.drain()
            v.tensor_scalar(out=tc1, in0=mc, scalar1=mc,
                            scalar2=None, op0=ALU.mult)
            v.drain()
            v.tensor_scalar(out=vc, in0=vc, scalar1=tc1,
                            scalar2=1e-5, op0=ALU.subtract, op1=ALU.add)
            i1 = v.drain()
            MK[("var", tau)] = inc(i1, "vt")

        @block.scalar
        def _(s, tau=tau, vc=vc, tc1=tc1):
            wge(s, "vt", MK[("var", tau)])
            i1 = s.activation(tc1, vc, AF.Sqrt)
            MK[("sd", tau)] = inc(i1, "sl")

        @block.vector
        def _(v, tau=tau, mc=mc, rc=rc, tc1=tc1, htc=htc):
            wge(v, "sl", MK[("sd", tau)])
            v.reciprocal(rc, tc1)
            v.drain()
            v.tensor_scalar(out=htc[:, :], in0=xs[:, tau, :], scalar1=mc,
                            scalar2=rc, op0=ALU.subtract, op1=ALU.mult)
            v.drain()
            v.tensor_tensor(htc[:, :], htc[:, :], bcA[:, :], ALU.mult)
            v.drain()
            i1 = v.tensor_tensor(htc[:, :], htc[:, :], bcB[:, :], ALU.add)
            MK[("h", tau)] = inc(i1, "vt")

        @block.scalar
        def _(s, tau=tau, sqhc=sqhc, htc=htc):
            wge(s, "vt", MK[("h", tau)])
            i1 = s.activation(sqd, htc[:, :], AF.Square, accum_out=sqhc)
            MK[("sqh", tau)] = inc(i1, "sl")

        @block.vector
        def _(v, tau=tau, sqhc=sqhc, tc1=tc1):
            wge(v, "sl", MK[("sqh", tau)])
            v.tensor_scalar(out=tc1, in0=sqhc, scalar1=1.0 / D,
                            scalar2=1e-6, op0=ALU.mult, op1=ALU.add)
            i1 = v.drain()
            MK[("ms", tau)] = inc(i1, "vt")

        @block.scalar
        def _(s, tau=tau, tc1=tc1):
            wge(s, "vt", MK[("ms", tau)])
            i1 = s.activation(tc1, tc1, AF.Sqrt)
            MK[("sd2", tau)] = inc(i1, "sl")

        @block.vector
        def _(v, tau=tau, rrc=rrc, tc1=tc1, htc=htc):
            wge(v, "sl", MK[("sd2", tau)])
            v.reciprocal(rrc, tc1)
            v.drain()
            if tau >= 2:
                wge(v, "mt", MK[("fmT", tau - 2)])
            i1 = v.tensor_scalar(out=xnb[:, tau % 2, :], in0=htc[:, :],
                                 scalar1=rrc, scalar2=None, op0=ALU.mult)
            MK[("xn", tau)] = inc(i1, "vt")

        @block.tensor
        def _(t, tau=tau):
            wge(t, "vt", MK[("xn", tau)])
            if tau == 0:
                wge(t, "mt", M_END_MT)
            else:
                wge(t, "vt", MK[("fmev", tau - 1)])
            for q in range(4):
                t.matmul(ps[0][:, 128 * q:128 * (q + 1)],
                         xnb[:, tau % 2, 128 * q:128 * (q + 1)],
                         idb[:, :], start=True, stop=True)
            for q in range(4):
                i1 = t.matmul(ps[1][:, 128 * q:128 * (q + 1)],
                              xnb[:, tau % 2, 128 * (4 + q):128 * (5 + q)],
                              idb[:, :], start=True, stop=True)
            MK[("fmT", tau)] = inc(i1, "mt")

        @block.vector
        def _(v, tau=tau):
            wge(v, "mt", MK[("fmT", tau)])
            v.tensor_copy(fm[:, 0:4, 128 * tau:128 * (tau + 1)],
                          ps[0].ap().rearrange("p (k t) -> p k t", t=P))
            i1 = v.tensor_copy(fm[:, 4:8, 128 * tau:128 * (tau + 1)],
                               ps[1].ap().rearrange("p (k t) -> p k t", t=P))
            MK[("fmev", tau)] = inc(i1, "vt")

        @block.tensor
        def _(t, tau=tau):
            wge(t, "vt", MK[("fmev", tau)])
            if tau >= 1:
                wge(t, "vt", MK[("qkev", tau - 1)])
            for kc in range(NCH):
                i1 = t.matmul(ps[2][:, 0:384],
                              fm[:, kc, 128 * tau:128 * (tau + 1)],
                              wqkvT[:, kc, :], start=(kc == 0),
                              stop=(kc == NCH - 1))
            MK[("qkv", tau)] = inc(i1, "mt")

        @block.vector
        def _(v, tau=tau):
            wge(v, "mt", MK[("qkv", tau)])
            i1 = v.tensor_copy(qkvsb[:, tau % 2, :], ps[2][:, 0:384])
            MK[("qkev", tau)] = inc(i1, "vt")

        @block.tensor
        def _(t, tau=tau):
            wge(t, "vt", MK[("qkev", tau)])
            qk = qkvsb[:, tau % 2, :]
            t.matmul(ps[6][:, 0:256], qk[:, 0:128], qk[:, 0:256],
                     start=(tau == 0), stop=(tau == NT - 1))
            t.matmul(ps[3][:, 0:256], qk[:, 128:256], qk[:, 0:256],
                     start=(tau == 0), stop=(tau == NT - 1))
            if tau >= 2:
                wge(t, "vt", MK[("vev", tau - 2)])
            i1 = t.matmul(ps[4 + tau % 2][:, 0:128], qk[:, 256:384],
                          idb[:, :], start=True, stop=True)
            MK[("vT", tau)] = inc(i1, "mt")

        @block.vector
        def _(v, tau=tau):
            wge(v, "mt", MK[("vT", tau)])
            i1 = v.tensor_copy(vfm[:, 128 * tau:128 * (tau + 1)],
                               ps[4 + tau % 2][:, 0:128])
            MK[("vev", tau)] = inc(i1, "vt")

    @block.vector
    def _(v):
        wge(v, "mt", MK[("vT", NT - 1)])
        v.tensor_tensor(Smat[:, :], ps[6][:, 0:128], idf[:, :], ALU.mult)
        v.drain()
        v.reduce_sum(qn[:, :], Smat[:, :], AX)
        v.drain()
        v.tensor_tensor(Smat[:, :], ps[3][:, 128:256], idf[:, :], ALU.mult)
        v.drain()
        v.reduce_sum(kn[:, :], Smat[:, :], AX)
        v.drain()
        v.tensor_copy(arst[:, 0:128], ps[3][:, 0:128])
        v.tensor_copy(arst[:, 128:129], qn[:, :])
        i1 = v.tensor_copy(arst[:, 129:130], kn[:, :])
        inc(i1, "vt")
    V_ARPACK = cnt["vt"]

    @block.gpsimd
    def _(g):
        wge(g, "vt", V_ARPACK)
        inc(g.dma_start(out=cc_in.ap(), in_=arst[:, :]), "la", 16)
        wge(g, "la", cnt["la"])
        i1 = g.collective_compute(
            "AllReduce", ALU.add,
            replica_groups=[list(range(NCORES))],
            ins=[cc_in.ap().opt()],
            outs=[cc_out.ap().opt()])
        inc(i1, "cc")
        wge(g, "cc", cnt["cc"])
        inc(g.dma_start(out=aro[:, :], in_=cc_out.ap()), "la", 16)
    LD_AR = cnt["la"]

    unit_ld_marks = []

    @block.gpsimd
    def _(g):
        for u in range(NUNITS):
            w1b, w3b, w2b = (wA1, wA3, wA2) if u % 2 == 0 else (wB1, wB3, wB2)
            if u < 2:
                wge(g, "wm", NSLC)
            else:
                wge(g, "gf", u - 1)
            if u < 2:
                src1 = d_sw1[:, EI * u:EI * (u + 1)]
                src3 = d_sw3[:, EI * u:EI * (u + 1)]
                src2 = d_sw2[EI * u:EI * (u + 1), :]
            else:
                e = u - 2
                src1, src3, src2 = d_w1[e], d_w3[e], d_w2[e]
            sn = "luA" if u % 2 == 0 else "luB"
            inc(g.dma_start(out=w1b[:, :, :],
                            in_=src1.rearrange("(k p) i -> p k i", p=P)), sn, 16)
            inc(g.dma_start(out=w3b[:, :, :],
                            in_=src3.rearrange("(k p) i -> p k i", p=P)), sn, 16)
            i1 = g.dma_start(out=w2b[:, :, :],
                             in_=src2.rearrange("(k p) j -> p k j", p=P))
            unit_ld_marks.append((sn, inc(i1, sn, 16)))

    @block.vector
    def _(v):
        wge(v, "la", LD_AR)
        v.tensor_copy(qn[:, :], aro[:, 128:129])
        i1 = v.tensor_copy(kn[:, :], aro[:, 129:130])
        v.drain()
        i1 = v.nop()
        inc(i1, "vt")
    V_ARQK = cnt["vt"]

    @block.scalar
    def _(s):
        wge(s, "vt", V_ARQK)
        s.activation(qn[:, :], qn[:, :], AF.Sqrt)
        i1 = s.activation(kn[:, :], kn[:, :], AF.Sqrt)
        inc(i1, "sl")
    SL_SQRTN = cnt["sl"]

    @block.vector
    def _(v):
        wge(v, "sl", SL_SQRTN)
        v.tensor_scalar_max(qn[:, :], qn[:, :], 1e-12)
        v.tensor_scalar_max(kn[:, :], kn[:, :], 1e-12)
        v.drain()
        v.reciprocal(rq[:, :], qn[:, :])
        v.reciprocal(rk[:, :], kn[:, :])
        v.drain()
        i1 = v.tensor_scalar(out=A1b[:, :], in0=aro[:, 0:128],
                             scalar1=rk[:, :], scalar2=None, op0=ALU.mult)
        inc(i1, "vt")
    V_A1 = cnt["vt"]

    @block.tensor
    def _(t):
        wge(t, "vt", V_A1)
        i1 = t.matmul(ps[7][:, 0:128], A1b[:, :], idb[:, :], start=True,
                      stop=True)
        MK[("A1T",)] = inc(i1, "mt")

    @block.vector
    def _(v):
        wge(v, "mt", MK[("A1T",)])
        v.tensor_scalar(out=Smat[:, :], in0=ps[7][:, 0:128], scalar1=rq[:, :],
                        scalar2=texpc[:, :], op0=ALU.mult, op1=ALU.mult)
        v.drain()
        v.tensor_tensor(Smat[:, :], Smat[:, :], blkm[:, :], ALU.add)
        v.drain()
        v.reduce_max(nmx[:, 0:1], Smat[:, :], AX, negate=True)
        i1 = v.drain()
        inc(i1, "vt")
    V_SMAT = cnt["vt"]

    @block.scalar
    def _(s):
        wge(s, "vt", V_SMAT)
        i1 = s.activation(Pmat[:, :], Smat[:, :], AF.Exp, bias=nmx[:, 0:1])
        inc(i1, "sl")
    SL_PEXP = cnt["sl"]

    @block.vector
    def _(v):
        wge(v, "sl", SL_PEXP)
        v.reduce_sum(sden[:, 0:1], Pmat[:, :], AX)
        v.drain()
        v.reciprocal(sden[:, 0:1], sden[:, 0:1])
        v.drain()
        i1 = v.tensor_scalar(out=Pbf[:, :], in0=Pmat[:, :], scalar1=sden[:, 0:1],
                             scalar2=None, op0=ALU.mult)
        inc(i1, "vt")
    V_PBF = cnt["vt"]

    @block.tensor
    def _(t):
        wge(t, "vt", V_PBF)
        i1 = t.matmul(ps[7][:, 0:128], Pbf[:, :], idb[:, :], start=True,
                      stop=True)
        MK[("PT",)] = inc(i1, "mt")

    @block.vector
    def _(v):
        wge(v, "mt", MK[("PT",)])
        i1 = v.tensor_copy(PTb[:, :], ps[7][:, 0:128])
        inc(i1, "vt")
    V_PT = cnt["vt"]

    for tc in range(4):
        @block.tensor
        def _(t, tc=tc):
            wge(t, "vt", V_PT)
            if tc >= 2:
                wge(t, "vt", MK[("avev", tc - 2)])
            i1 = t.matmul(ps[4 + tc % 2][:, :], PTb[:, :],
                          vfm[:, 512 * tc:512 * (tc + 1)], start=True,
                          stop=True)
            MK[("av", tc)] = inc(i1, "mt")

        @block.vector
        def _(v, tc=tc):
            wge(v, "mt", MK[("av", tc)])
            i1 = v.tensor_copy(avfm[:, 512 * tc:512 * (tc + 1)],
                               ps[4 + tc % 2][:, :])
            MK[("avev", tc)] = inc(i1, "vt")

    for tau in range(NT):
        @block.tensor
        def _(t, tau=tau):
            wge(t, "vt", MK[("avev", tau // 4)])
            if tau >= 1:
                wge(t, "vt", MK[("opev", tau - 1)])
            for cj in range(2):
                i1 = t.matmul(ps[cj][:, :], avfm[:, 128 * tau:128 * (tau + 1)],
                              woT[:, 512 * cj:512 * (cj + 1)], start=True,
                              stop=True)
            MK[("op", tau)] = inc(i1, "mt")

        @block.vector
        def _(v, tau=tau):
            wge(v, "mt", MK[("op", tau)])
            v.tensor_tensor(xs[:, tau, 0:512], xs[:, tau, 0:512],
                            ps[0][:, :], ALU.add)
            i1 = v.tensor_tensor(xs[:, tau, 512:1024], xs[:, tau, 512:1024],
                                 ps[1][:, :], ALU.add)
            MK[("opev", tau)] = inc(i1, "vt")

    @block.tensor
    def _(t):
        wge(t, "vt", MK[("opev", NT - 1)])
        i1 = t.matmul(ps[6][:, 0:E], ones_f[0:1, 0:P], gbg_row[0:1, :],
                      start=True, stop=True)
        MK[("gbg",)] = inc(i1, "mt")

    @block.vector
    def _(v):
        wge(v, "mt", MK[("gbg",)])
        i1 = v.tensor_copy(gbg_bc[:, :], ps[6][:, 0:E])
        inc(i1, "vt")
    V_GBG = cnt["vt"]

    for tau in range(NT):
        pc = tau % 2
        smc = sm[:, pc:pc + 1]
        sqxc = sqx[:, pc:pc + 1]
        mc = mcol[:, pc:pc + 1]
        vc = vcol[:, pc:pc + 1]
        rc = rcol[:, pc:pc + 1]
        tc1 = tcol[:, pc:pc + 1]
        htc = ht if pc == 0 else ht2
        nmc = nmx[:, pc:pc + 1]
        sdc = sden[:, pc:pc + 1]

        @block.vector
        def _(v, tau=tau, smc=smc):
            if tau == 0:
                wge(v, "vt", V_GBG)
            v.reduce_sum(smc, xs[:, tau, :], AX)
            i1 = v.drain()
            MK[("sm2", tau)] = inc(i1, "vt")

        @block.scalar
        def _(s, tau=tau, sqxc=sqxc):
            if tau == 0:
                wge(s, "vt", MK[("opev", NT - 1)])
            i1 = s.activation(sqd, xs[:, tau, :], AF.Square,
                              accum_out=sqxc)
            MK[("sqx2", tau)] = inc(i1, "sl")

        @block.vector
        def _(v, tau=tau, smc=smc, sqxc=sqxc, mc=mc, vc=vc, tc1=tc1):
            wge(v, "sl", MK[("sqx2", tau)])
            v.tensor_scalar(out=mc, in0=smc, scalar1=1.0 / D,
                            scalar2=None, op0=ALU.mult)
            v.tensor_scalar(out=vc, in0=sqxc, scalar1=1.0 / D,
                            scalar2=None, op0=ALU.mult)
            v.drain()
            v.tensor_scalar(out=tc1, in0=mc, scalar1=mc,
                            scalar2=None, op0=ALU.mult)
            v.drain()
            v.tensor_scalar(out=vc, in0=vc, scalar1=tc1,
                            scalar2=1e-5, op0=ALU.subtract, op1=ALU.add)
            i1 = v.drain()
            MK[("var2", tau)] = inc(i1, "vt")

        @block.scalar
        def _(s, tau=tau, vc=vc, tc1=tc1):
            wge(s, "vt", MK[("var2", tau)])
            i1 = s.activation(tc1, vc, AF.Sqrt)
            MK[("sdb", tau)] = inc(i1, "sl")

        @block.vector
        def _(v, tau=tau, mc=mc, rc=rc, tc1=tc1, htc=htc):
            wge(v, "sl", MK[("sdb", tau)])
            v.reciprocal(rc, tc1)
            v.drain()
            v.tensor_scalar(out=htc[:, :], in0=xs[:, tau, :], scalar1=mc,
                            scalar2=rc, op0=ALU.subtract, op1=ALU.mult)
            v.drain()
            v.tensor_tensor(htc[:, :], htc[:, :], bcA2[:, :], ALU.mult)
            v.drain()
            v.tensor_tensor(htc[:, :], htc[:, :], bcB2[:, :], ALU.add)
            v.drain()
            if tau >= 2:
                wge(v, "mt", MK[("fmT2", tau - 2)])
            i1 = v.tensor_copy(xnb[:, tau % 2, :], htc[:, :])
            MK[("h2", tau)] = inc(i1, "vt")

        @block.tensor
        def _(t, tau=tau):
            wge(t, "vt", MK[("h2", tau)])
            if tau >= 1:
                wge(t, "vt", MK[("fmev2", tau - 1)])
            for q in range(4):
                t.matmul(ps[0][:, 128 * q:128 * (q + 1)],
                         xnb[:, tau % 2, 128 * q:128 * (q + 1)],
                         idb[:, :], start=True, stop=True)
            for q in range(4):
                i1 = t.matmul(ps[1][:, 128 * q:128 * (q + 1)],
                              xnb[:, tau % 2, 128 * (4 + q):128 * (5 + q)],
                              idb[:, :], start=True, stop=True)
            MK[("fmT2", tau)] = inc(i1, "mt")

        @block.vector
        def _(v, tau=tau):
            wge(v, "mt", MK[("fmT2", tau)])
            v.tensor_copy(fm[:, 0:4, 128 * tau:128 * (tau + 1)],
                          ps[0].ap().rearrange("p (k t) -> p k t", t=P))
            i1 = v.tensor_copy(fm[:, 4:8, 128 * tau:128 * (tau + 1)],
                               ps[1].ap().rearrange("p (k t) -> p k t", t=P))
            MK[("fmev2", tau)] = inc(i1, "vt")

        @block.tensor
        def _(t, tau=tau):
            wge(t, "vt", MK[("fmev2", tau)])
            if tau >= 1:
                wge(t, "vt", MK[("gtt", tau - 1)])
            for kc in range(NCH):
                i1 = t.matmul(ps[2][:, 0:E],
                              fm[:, kc, 128 * tau:128 * (tau + 1)],
                              gwsb[:, kc, :], start=(kc == 0),
                              stop=(kc == NCH - 1))
            MK[("gmm", tau)] = inc(i1, "mt")

        @block.vector
        def _(v, tau=tau, nmc=nmc):
            wge(v, "mt", MK[("gmm", tau)])
            v.tensor_tensor(gsc[:, :], ps[2][:, 0:E], gbg_bc[:, :], ALU.add)
            v.drain()
            v.reduce_max(nmc, gsc[:, :], AX, negate=True)
            i1 = v.drain()
            MK[("gtt", tau)] = inc(i1, "vt")

        @block.scalar
        def _(s, tau=tau, nmc=nmc):
            wge(s, "vt", MK[("gtt", tau)])
            i1 = s.activation(gsc[:, :], gsc[:, :], AF.Exp, bias=nmc)
            MK[("gexp", tau)] = inc(i1, "sl")

        @block.vector
        def _(v, tau=tau, sdc=sdc):
            wge(v, "sl", MK[("gexp", tau)])
            v.reduce_sum(sdc, gsc[:, :], AX)
            v.drain()
            v.reciprocal(sdc, sdc)
            v.drain()
            v.tensor_scalar(out=gsc[:, :], in0=gsc[:, :], scalar1=sdc,
                            scalar2=None, op0=ALU.mult)
            v.drain()
            a0 = gsc.ap().rearrange("p (g f) -> p g f", f=4)
            first = True
            for (i, j) in [(0, 1), (0, 2), (0, 3), (1, 2), (1, 3), (2, 3)]:
                v.tensor_tensor(gt8[:, :], a0[:, :, i], a0[:, :, j], ALU.add)
                v.drain()
                if first:
                    v.tensor_copy(gs8[:, :], gt8[:, :])
                    first = False
                else:
                    v.tensor_tensor(gs8[:, :], gs8[:, :], gt8[:, :], ALU.max)
                v.drain()
            v.max(out=gt8[:, :], in_=gs8[:, :])
            v.drain()
            v.tensor_scalar(out=gs8[:, :], in0=gs8[:, :], scalar1=gt8[:, 3:4],
                            scalar2=None, op0=ALU.is_ge)
            v.drain()
            v.tensor_copy(gmk.ap().rearrange("p (g f) -> p g f", f=4),
                          gs8.ap().rearrange("p (g f) -> p g f", f=1)
                          .broadcast_to([P, 8, 4]))
            v.drain()
            v.tensor_scalar(out=invm[:, :], in0=gmk[:, :], scalar1=0.5,
                            scalar2=None, op0=ALU.is_lt)
            v.drain()
            v.tensor_copy(gmk[:, :], gsc[:, :])
            v.drain()
            v.copy_predicated(gmk[:, :], invm[:, :], negbig[:, :])
            v.drain()
            v.max(out=gt8[:, :], in_=gmk[:, :])
            v.drain()
            v.tensor_scalar(out=gsel[:, :], in0=gmk[:, :], scalar1=gt8[:, 3:4],
                            scalar2=None, op0=ALU.is_ge)
            v.drain()
            i1 = v.tensor_tensor(cw[:, tau, :], gsel[:, :], gsc[:, :], ALU.mult)
            MK[("cw", tau)] = inc(i1, "vt")

        @block.tensor
        def _(t, tau=tau):
            wge(t, "vt", MK[("cw", tau)])
            if tau >= 2:
                wge(t, "vt", MK[("cwTev", tau - 2)])
            i1 = t.matmul(ps[3 + tau % 2][0:E, 0:128], cw[:, tau, :],
                          idf[:, :], start=True, stop=True)
            MK[("cwT", tau)] = inc(i1, "mt")

        @block.vector
        def _(v, tau=tau):
            wge(v, "mt", MK[("cwT", tau)])
            i1 = v.tensor_copy(cwT[0:E, 128 * tau:128 * (tau + 1)],
                               ps[3 + tau % 2][0:E, 0:128])
            MK[("cwTev", tau)] = inc(i1, "vt")

    @block.sync
    def _(sy):
        wge(sy, "vt", MK[("h2", NT - 1)])
        inc(sy.dma_start(out=ht[0:E, :], in_=d_b2), "ls", 16)
        i1 = sy.dma_start(out=ht[E:E + 1, :],
                          in_=d_sb2.rearrange("(a c) -> a c", a=1))
        inc(i1, "ls", 16)
    LS_B2 = cnt["ls"]

    @block.vector
    def _(v):
        wge(v, "ls", LS_B2)
        wge(v, "vt", bc_marks[23])
        i1 = v.tensor_tensor(b2gb[:, :], ht[0:E + 1, :], bcG[0:E + 1, :],
                             ALU.mult)
        inc(i1, "vt")
    V_B2G = cnt["vt"]
    C_END_VT = cnt["vt"]

    setA = [ps[0], ps[1], ps[2], ps[3]]
    setB = [ps[4], ps[5], ps[6], ps[7]]

    def unit_bufs(u):
        return (wA1, wA3, wA2) if u % 2 == 0 else (wB1, wB3, wB2)

    def emit_w1(u, i):
        w1b, _, _ = unit_bufs(u)

        @block.tensor
        def _(t):
            st = setA if i % 2 == 0 else setB
            if i == 0:
                wge(t, unit_ld_marks[u][0], unit_ld_marks[u][1])
            if u == 0 and i < 2:
                wge(t, "vt", C_END_VT)
            elif i < 2:
                wge(t, "gf", u)
            else:
                wge(t, "vt", MK[("tt", u, i - 2)])
            for c in range(NCH):
                for tcc in range(4):
                    i1 = t.matmul(st[tcc][:, :],
                                  w1b[:, c, 128 * i:128 * (i + 1)],
                                  fm[:, c, 512 * tcc:512 * (tcc + 1)],
                                  start=(c == 0), stop=(c == NCH - 1))
            MK[("w1", u, i)] = inc(i1, "mt")

    def emit_silu(u, i):
        shared = u < 2

        @block.scalar
        def _(s):
            st0 = 0 if i % 2 == 0 else 4
            wge(s, "mt", MK[("w1", u, i)])
            if i >= 2:
                wge(s, "vt", MK[("tt", u, i - 2)])
            elif u >= 1:
                wge(s, "vt", MK[("tt", u - 1, i + 2)])
            bias = (sb1T[:, 4 * u + i:4 * u + i + 1] if shared
                    else b1T[:, i, (u - 2):(u - 1)])
            for tcc in range(4):
                i1 = s.activation(s1[:, i % 2, tcc, :], ps[st0 + tcc][:, :],
                                  AF.Silu, bias=bias)
            MK[("si", u, i)] = inc(i1, "sl")

    def emit_w3(u, i):
        _, w3b, _ = unit_bufs(u)

        @block.tensor
        def _(t):
            st = setA if i % 2 == 0 else setB
            wge(t, "sl", MK[("si", u, i)])
            for c in range(NCH):
                for tcc in range(4):
                    i1 = t.matmul(st[tcc][:, :],
                                  w3b[:, c, 128 * i:128 * (i + 1)],
                                  fm[:, c, 512 * tcc:512 * (tcc + 1)],
                                  start=(c == 0), stop=(c == NCH - 1))
            MK[("w3", u, i)] = inc(i1, "mt")

    def emit_tt(u, i):
        shared = u < 2

        @block.vector
        def _(v):
            st0 = 0 if i % 2 == 0 else 4
            wge(v, "mt", MK[("w3", u, i)])
            if u >= 1 and i == 0:
                wge(v, "mt", MK[("w2", u - 1, NT - 1)])
            b3c = (sb3T[:, 4 * u + i:4 * u + i + 1] if shared
                   else b3T[:, i, (u - 2):(u - 1)])
            for tcc in range(4):
                i1 = v.scalar_tensor_tensor(
                    out=h1[:, i, 512 * tcc:512 * (tcc + 1)],
                    in0=ps[st0 + tcc][:, :], scalar=b3c,
                    in1=s1[:, i % 2, tcc, :], op0=ALU.add, op1=ALU.mult)
            MK[("tt", u, i)] = inc(i1, "vt")

    def emit_w2fold(u):
        _, _, w2b = unit_bufs(u)

        @block.vector
        def _(v):
            wge(v, unit_ld_marks[u][0], unit_ld_marks[u][1])
            if u == 0:
                wge(v, "vt", bc_marks[23])
            i1 = v.tensor_tensor(
                w2b[:, :, :], w2b[:, :, :],
                bcG.ap().rearrange("p (a c) -> p a c", a=1)
                .broadcast_to([P, 4, D]), ALU.mult)
            MK[("w2f", u)] = inc(i1, "vt")

    def emit_w2_group(u, g):
        _, _, w2b = unit_bufs(u)

        @block.tensor
        def _(t):
            if g == 0:
                wge(t, "vt", MK[("tt", u, 3)])
                wge(t, "vt", MK[("w2f", u)])
            for tau in range(2 * g, 2 * g + 2):
                pa = ps[0 + 2 * (tau % 4)]
                pb = ps[1 + 2 * (tau % 4)]
                if tau >= 4:
                    wge(t, "vt", MK[("ev", u, tau - 4)])
                for i in range(4):
                    t.matmul(pa[:, :], h1[:, i, 128 * tau:128 * (tau + 1)],
                             w2b[:, i, 0:512], start=(i == 0), stop=(i == 3))
                    i1 = t.matmul(pb[:, :],
                                  h1[:, i, 128 * tau:128 * (tau + 1)],
                                  w2b[:, i, 512:1024], start=(i == 0),
                                  stop=(i == 3))
                MK[("w2", u, tau)] = inc(i1, "mt")

    def emit_ev_group(u, g):
        shared = u < 2

        @block.vector
        def _(v):
            for tau in range(2 * g, 2 * g + 2):
                pa = ps[0 + 2 * (tau % 4)]
                pb = ps[1 + 2 * (tau % 4)]
                wge(v, "mt", MK[("w2", u, tau)])
                if shared:
                    v.tensor_tensor(xs[:, tau, 0:512], xs[:, tau, 0:512],
                                    pa[:, :], ALU.add)
                    i1 = v.tensor_tensor(xs[:, tau, 512:1024],
                                         xs[:, tau, 512:1024], pb[:, :],
                                         ALU.add)
                else:
                    cws = cw[:, tau, (u - 2):(u - 1)]
                    v.scalar_tensor_tensor(
                        out=xs[:, tau, 0:512], in0=pa[:, :], scalar=cws,
                        in1=xs[:, tau, 0:512], op0=ALU.mult, op1=ALU.add)
                    i1 = v.scalar_tensor_tensor(
                        out=xs[:, tau, 512:1024], in0=pb[:, :], scalar=cws,
                        in1=xs[:, tau, 512:1024], op0=ALU.mult, op1=ALU.add)
                if tau == NT - 1:
                    inc(i1, "gf")
                else:
                    MK[("ev", u, tau)] = inc(i1, "vt")

    for u in range(NUNITS):
        emit_w2fold(u)
        emit_w1(u, 0)
        emit_silu(u, 0)
        emit_w1(u, 1)
        emit_silu(u, 1)
        emit_w3(u, 0)
        emit_tt(u, 0)
        emit_w1(u, 2)
        emit_silu(u, 2)
        emit_w3(u, 1)
        emit_tt(u, 1)
        emit_w1(u, 3)
        emit_silu(u, 3)
        emit_w3(u, 2)
        emit_tt(u, 2)
        emit_w3(u, 3)
        emit_tt(u, 3)
        for g in range(8):
            emit_w2_group(u, g)
            emit_ev_group(u, g)

    for g in range(8):
        @block.tensor
        def _(t, g=g):
            if g == 0:
                wge(t, "vt", MK[("cwTev", NT - 1)])
                wge(t, "vt", V_B2G)
            for tau in range(2 * g, 2 * g + 2):
                pa = ps[0 + 2 * (tau % 2)]
                pb = ps[1 + 2 * (tau % 2)]
                if tau < 2:
                    wge(t, "gf", NUNITS)
                else:
                    wge(t, "vt", MK[("b2ev", tau - 2)])
                t.matmul(pa[:, :], cwT[:, 128 * tau:128 * (tau + 1)],
                         b2gb[:, 0:512], start=True, stop=True)
                i1 = t.matmul(pb[:, :], cwT[:, 128 * tau:128 * (tau + 1)],
                              b2gb[:, 512:1024], start=True, stop=True)
                MK[("b2mm", tau)] = inc(i1, "mt")

        @block.vector
        def _(v, g=g):
            for tau in range(2 * g, 2 * g + 2):
                pa = ps[0 + 2 * (tau % 2)]
                pb = ps[1 + 2 * (tau % 2)]
                wge(v, "mt", MK[("b2mm", tau)])
                v.tensor_tensor(xs[:, tau, 0:512], xs[:, tau, 0:512],
                                pa[:, :], ALU.add)
                i1 = v.tensor_tensor(xs[:, tau, 512:1024],
                                     xs[:, tau, 512:1024], pb[:, :], ALU.add)
                MK[("b2ev", tau)] = inc(i1, "vt")
    V_FINAL = cnt["vt"]

    @block.sync
    def _(sy):
        wge(sy, "vt", V_FINAL)
        inc(sy.dma_start(out=d_out.rearrange("(t p) c -> p t c", p=P),
                         in_=xs[:, :, :]), "st", 16)
        wge(sy, "st", cnt["st"])

    blk_es.close()
    es.close()
    return nc


_CACHE = {}


def kernel(**inputs):
    nc = _CACHE.get("nc")
    if nc is None:
        nc = build_nc()
        _CACHE["nc"] = nc

    x = np.ascontiguousarray(
        np.asarray(inputs["L0_attn_x"], dtype=np.float32).reshape(16384, D))
    full = {
        "attn_nw": np.ascontiguousarray(inputs["L0_attn_norm_weight"], dtype=np.float32),
        "attn_wqkv": np.ascontiguousarray(inputs["L0_attn_w_qkv"], dtype=np.float32),
        "attn_temp": np.ascontiguousarray(
            np.asarray(inputs["L0_attn_temperature"], dtype=np.float32).reshape(8)),
        "attn_wout": np.ascontiguousarray(inputs["L0_attn_w_out"], dtype=np.float32),
        "ffn_gw": np.ascontiguousarray(inputs["L0_ffn_gw"], dtype=np.float32),
        "ffn_gb": np.ascontiguousarray(inputs["L0_ffn_gb"], dtype=np.float32),
        "ffn_w1": np.ascontiguousarray(inputs["L0_ffn_w1"], dtype=np.float32),
        "ffn_b1": np.ascontiguousarray(inputs["L0_ffn_b1"], dtype=np.float32),
        "ffn_w3": np.ascontiguousarray(inputs["L0_ffn_w3"], dtype=np.float32),
        "ffn_b3": np.ascontiguousarray(inputs["L0_ffn_b3"], dtype=np.float32),
        "ffn_w2": np.ascontiguousarray(inputs["L0_ffn_w2"], dtype=np.float32),
        "ffn_b2": np.ascontiguousarray(inputs["L0_ffn_b2"], dtype=np.float32),
        "ffn_sw1": np.ascontiguousarray(inputs["L0_ffn_sw1"], dtype=np.float32),
        "ffn_sb1": np.ascontiguousarray(inputs["L0_ffn_sb1"], dtype=np.float32),
        "ffn_sw3": np.ascontiguousarray(inputs["L0_ffn_sw3"], dtype=np.float32),
        "ffn_sb3": np.ascontiguousarray(inputs["L0_ffn_sb3"], dtype=np.float32),
        "ffn_sw2": np.ascontiguousarray(inputs["L0_ffn_sw2"], dtype=np.float32),
        "ffn_sb2": np.ascontiguousarray(inputs["L0_ffn_sb2"], dtype=np.float32),
        "cvec": np.ascontiguousarray(
            np.asarray(inputs["L0_c"], dtype=np.float32).reshape(1, D)),
        "wmod": np.ascontiguousarray(inputs["L0_wmod"], dtype=np.float32),
    }
    in_maps = []
    for c in range(NCORES):
        m = dict(full)
        m["xsh"] = np.ascontiguousarray(x[c * TPC:(c + 1) * TPC])
        in_maps.append(m)
    res = run_bass_kernel_spmd(nc, in_maps, core_ids=list(range(NCORES)))
    out = np.concatenate([np.asarray(res.results[c]["out"])
                          for c in range(NCORES)], axis=0)
    return out.reshape(1, 16384, D)
